# version 05
# baseline (speedup 1.0000x reference)
import numpy as np
import ml_dtypes
import concourse.bass as bass
import concourse.mybir as mybir
from concourse.bass_utils import run_bass_kernel_spmd
from contextlib import ExitStack

F32 = mybir.dt.float32
BF16 = mybir.dt.bfloat16
AF = mybir.ActivationFunctionType
OP = mybir.AluOpType
AXX = mybir.AxisListType.X

NCORE = 8
L = 8192
T = 1024
NCH = 8
H = 32
KVH = 8
D = 128
HID = 4096
NI = 32
ROPE_THETA = 10000.0

WA_COLS = 4096 + 1024 + 1024 + 128 + 512 + 128
ALL_PHASES = ("A0", "A1", "B1", "CC", "C1a", "B2", "C1b", "C2")
ENGS = ("tensor", "vector", "scalar", "gpsimd", "sync")


class Prog:

    def __init__(self, nc, stack):
        self.nc = nc
        self.stream = {e: [] for e in ENGS}
        self.msem = {e: stack.enter_context(nc.semaphore(f"ms_{e}")) for e in ENGS}
        self.mcount = {e: 0 for e in ENGS}
        self.waited = {e: {} for e in ENGS}
        self.dma_pool = {}
        for e in ("sync", "scalar", "gpsimd"):
            self.dma_pool[e] = [stack.enter_context(nc.semaphore(f"dq_{e}{i}"))
                                for i in range(8)]
        self.dma_n = {e: 0 for e in ("sync", "scalar", "gpsimd")}

    def _waits(self, eng, deps):
        out = []
        for d in deps:
            if d is None:
                continue
            sem, val = d
            key = sem.num if hasattr(sem, "num") else id(sem)
            if self.waited[eng].get(key, 0) < val:
                self.waited[eng][key] = val
                out.append((sem, val))
        return out

    SERIALIZE = {"vector": True, "scalar": True, "gpsimd": True,
                 "tensor": False, "sync": False}

    def op(self, eng, fn, deps=(), sig=False, par=1):
        ser = self.SERIALIZE[eng]
        deps = list(deps)
        if ser and self.mcount[eng] - (par - 1) > 0:
            deps.append((self.msem[eng], self.mcount[eng] - (par - 1)))
        waits = self._waits(eng, deps)
        tok = None
        if sig or ser:
            self.mcount[eng] += 1
            tok = (self.msem[eng], self.mcount[eng])
            msem = self.msem[eng]

            def clo(e, fn=fn, waits=waits, msem=msem):
                for s, v in waits:
                    e.wait_ge(s, v)
                fn(e).then_inc(msem, 1)
        else:
            def clo(e, fn=fn, waits=waits):
                for s, v in waits:
                    e.wait_ge(s, v)
                fn(e)
        self.stream[eng].append(clo)
        return tok

    def dma(self, eng, out, in_, deps=(), **kw):
        i = self.dma_n[eng]
        self.dma_n[eng] += 1
        sem = self.dma_pool[eng][i % 8]
        use = i // 8 + 1
        waits = self._waits(eng, deps)
        selfwait = (use - 1) * 16

        def clo(e, out=out, in_=in_, waits=waits, sem=sem, sw=selfwait, kw=kw):
            if sw > 0:
                e.wait_ge(sem, sw)
            for s, v in waits:
                e.wait_ge(s, v)
            e.dma_start(out=out, in_=in_, **kw).then_inc(sem, 16)
        self.stream[eng].append(clo)
        return (sem, use * 16)

    def qtoks(self, eng):
        n = self.dma_n[eng]
        toks = []
        for k in range(min(8, n)):
            i = n - 1 - k
            sem = self.dma_pool[eng][i % 8]
            toks.append((sem, (i // 8 + 1) * 16))
        return toks

    def build(self, block):
        for name in ENGS:
            items = self.stream[name]

            def make(items):
                def body(engine):
                    for clo in items:
                        clo(engine)
                return body
            getattr(block, name)(make(items))


def build_nc(phases=ALL_PHASES, dbg=(), c2_parts=(0, 1, 2)):
    nc = bass.Bass("TRN2", target_bir_lowering=False)
    phases = set(phases)
    dbg = set(dbg)

    def dram(name, shape, dtype):
        kind = "ExternalOutput" if name in dbg else "Internal"
        return nc.dram_tensor(name, shape, dtype, kind=kind).ap()

    x_in = nc.dram_tensor("x_c", [T, HID], F32, kind="ExternalInput").ap()
    WA = nc.dram_tensor("WA", [HID, WA_COLS], F32, kind="ExternalInput").ap()
    WG = nc.dram_tensor("WG", [HID, HID], F32, kind="ExternalInput").ap()
    WC = nc.dram_tensor("WC", [HID, 2 * HID], F32, kind="ExternalInput").ap()
    UPT_in = nc.dram_tensor("UPT", [512, HID], F32, kind="ExternalInput").ap()
    keysT_in = nc.dram_tensor("keysT", [128, 8], F32, kind="ExternalInput").ap()
    dtb_in = nc.dram_tensor("dt_bias", [32, 1], F32, kind="ExternalInput").ap()
    alog_in = nc.dram_tensor("A_log", [32, 1], F32, kind="ExternalInput").ap()
    gnorm_in = nc.dram_tensor("gnorm", [128, 1], F32, kind="ExternalInput").ap()
    cost_in = nc.dram_tensor("costab", [128, T], F32, kind="ExternalInput").ap()
    sint_in = nc.dram_tensor("sintab", [128, T], F32, kind="ExternalInput").ap()
    tri_in = nc.dram_tensor("trimask", [128, 128], F32, kind="ExternalInput").ap()
    ihead_in = nc.dram_tensor("Ihead", [32, HID], F32, kind="ExternalInput").ap()
    e127_in = nc.dram_tensor("e127", [128, 1], F32, kind="ExternalInput").ap()
    sel_in = nc.dram_tensor("Sel", [8, 512], F32, kind="ExternalInput").ap()
    cmf_in = nc.dram_tensor("cmflat", [1, 256], F32, kind="ExternalInput").ap()
    omcmf_in = nc.dram_tensor("omcmflat", [1, 256], F32, kind="ExternalInput").ap()
    id_in = nc.dram_tensor("ident_in", [128, 128], BF16, kind="ExternalInput").ap()
    idf_in = nc.dram_tensor("identf_in", [128, 128], F32, kind="ExternalInput").ap()
    pswap_in = nc.dram_tensor("pswapT", [128, 128], BF16, kind="ExternalInput").ap()

    out_t = nc.dram_tensor("out", [T, HID], F32, kind="ExternalOutput").ap()

    xn_tm = dram("xn_tm", [T, HID], BF16)
    q_sp = dram("q_sp", [HID, T], BF16)
    hT_sp = dram("hT_sp", [512, T], BF16)
    gs_sp = dram("gs_sp", [HID, T], BF16)
    g_bounce = dram("g_bounce", [NCH, HID], F32)
    r_bounce = dram("r_bounce", [4, 512], F32)
    pay = dram("pay", [128, 4160], F32)
    gath = dram("gath", [128 * NCORE, 4160], F32)
    upq = dram("upq", [512, HID], BF16)
    WG_bf = dram("WG_bf", [HID, HID], BF16)
    WC_bf = dram("WC_bf", [HID, 2 * HID], BF16)
    yh_dbg = dram("yh_dbg", [HID, T], BF16) if "yh_dbg" in dbg else None
    PAY_XS, PAY_P = 4096, 4128

    stack = ExitStack()
    with stack:
        def sb(name, shape, dtype, side="left", st=None):
            return (st or stack).enter_context(
                nc.sbuf_tensor(name, shape, dtype, side=side))

        def psum(name, shape, dtype):
            return stack.enter_context(nc.psum_tensor(name, shape, dtype))

        stack_r = stack.enter_context(ExitStack())

        def sb_r(name, shape, dtype):
            return sb(name, shape, dtype, side="right", st=stack_r)
        tri = sb("tri", [128, 128], F32)
        pswapT = sb("pswapT_s", [128, 128], BF16)
        tri_bf = sb("tri_bf", [128, 128], BF16)
        ident = sb("ident", [128, 128], BF16)
        identf = sb("identf", [128, 128], F32)
        gnorm = sb("gnormc", [128, 1], F32)
        e127 = sb("e127s", [128, 1], F32)
        ones_r = sb("ones_r", [1, 128], F32)
        consts = sb("consts", [128, 8], F32)
        C_I4096, C_EPS, C_I128, C_QS, C_ZERO, C_ONE = range(6)

        kT = sb_r("kT", [128, KVH * T], BF16)
        k_tm = sb_r("k_tm", [128, KVH * NCH * D], BF16)
        v_tm = sb_r("v_tm", [128, KVH * NCH * D], BF16)
        dt_tm = sb_r("dt_tm", [128, NCH * 32], F32)
        G_tm = sb_r("G_tm", [128, NCH * 32], F32)
        expG_tm = sb_r("expG_tm", [128, NCH * 32], F32)
        edec_tm = sb_r("edec_tm", [128, NCH * 32], F32)
        dec_bc = sb_r("dec_bc", [128, NCH * 32], F32)
        fut_bc = sb_r("fut_bc", [128, NCH * 32], F32)
        rows = sb_r("rows", [1, 576], F32)
        state = sb_r("state", [128, H * D], F32)
        xsum_c = sb_r("xsum_c", [128, 32], F32)
        acol = sb_r("acol", [32, 1], F32)
        dtbias = sb_r("dtbias", [32, 1], F32)
        stack_r2 = stack.enter_context(ExitStack())
        g_fm = sb("g_fm", [32, T], F32, side="right", st=stack_r2)
        dt_fm = sb("dt_fm", [32, T], F32, side="right", st=stack_r2)

        pb = [psum(f"pb{i}", [128, 512], F32) for i in range(6)]
        stack_pbt = stack.enter_context(ExitStack())
        pbt = [stack_pbt.enter_context(nc.psum_tensor("pbt0", [128, 1024], BF16)),
               stack_pbt.enter_context(nc.psum_tensor("pbt1", [128, 1024], BF16))]

        cc_sem = stack.enter_context(nc.semaphore("cc_sem"))
        blk = stack.enter_context(nc.Block())
        p = Prog(nc, stack)
        ctx = {"pbt_free": {0: None, 1: None}}

        t_tri = p.dma("sync", tri[:, :], tri_in)
        p.dma("sync", gnorm[:, :], gnorm_in)
        t_gn = p.dma("sync", e127[:, :], e127_in)
        t_dtb = p.dma("sync", dtbias[:, :], dtb_in)
        t_id = p.dma("sync", ident[:, :], id_in)
        t_idf = p.dma("sync", identf[:, :], idf_in)
        t_psw = p.dma("sync", pswapT[:, :], pswap_in)
        t_alog = p.dma("sync", acol[:, :], alog_in)
        t_tribf = p.op("vector", lambda e: e.tensor_copy(tri_bf[:, :], tri[:, :]),
                       deps=[t_tri], sig=True)
        p.op("vector", lambda e: e.memset(ones_r[:, :], 1.0))
        p.op("vector", lambda e: e.memset(consts[:, C_I4096:C_I4096 + 1], 1.0 / 4096))
        p.op("vector", lambda e: e.memset(consts[:, C_EPS:C_EPS + 1], 1e-5))
        p.op("vector", lambda e: e.memset(consts[:, C_I128:C_I128 + 1], 1.0 / 128))
        p.op("vector", lambda e: e.memset(
            consts[:, C_QS:C_QS + 1], 1.0 / (L * float(np.sqrt(128.0)))))
        p.op("vector", lambda e: e.memset(consts[:, C_ZERO:C_ZERO + 1], 0.0))
        t_consts = p.op("vector", lambda e: e.memset(
            consts[:, C_ONE:C_ONE + 1], 1.0), sig=True)
        t_aexp = p.op("scalar", lambda e: e.activation(
            acol[:, :], acol[:, :], AF.Exp), deps=[t_alog], sig=True)
        t_aneg = p.op("vector", lambda e: e.tensor_scalar(
            acol[:, :], acol[:, :], -1.0, None, OP.mult), deps=[t_aexp], sig=True)

        tok_xnT = []
        stack_xnT = stack.enter_context(ExitStack())
        xnT = sb("xnT", [128, NI * T], BF16, st=stack_xnT)
        stack_a = stack.enter_context(ExitStack())
        if "A0" in phases:
            xbuf = [sb(f"xbuf{i}", [128, HID], F32, st=stack_a)
                    for i in range(1)]
            xsq = sb("xsq", [128, HID // 4], F32, st=stack_a)
            xnb = sb("xnb", [128, HID], BF16, st=stack_a)
            lncol = sb("lncol", [128, 8], F32, st=stack_a)
            xn_w = []
            for t in range(8):
                b = xbuf[0]
                tl = p.dma("sync", b[:, :], x_in[t * 128:(t + 1) * 128, :],
                           deps=[xn_w[t - 1]] if t >= 1 else [])
                ts_ = p.op("vector", lambda e, b=b: e.tensor_reduce(
                    lncol[:, 0:1], b[:, :], AXX, OP.add), deps=[tl], sig=True)
                tprev = tl
                for qq in range(4):
                    tq = p.op("scalar", lambda e, b=b, qq=qq: e.activation(
                        xsq[:, :], b[:, qq * 1024:(qq + 1) * 1024], AF.Square),
                        deps=[tprev], sig=True)
                    tprev = p.op("vector", lambda e, qq=qq: e.tensor_reduce(
                        lncol[:, 4 + qq:5 + qq], xsq[:, :], AXX, OP.add),
                        deps=[tq], sig=True)
                p.op("vector", lambda e: e.tensor_add(
                    lncol[:, 4:5], lncol[:, 4:5], lncol[:, 5:6]))
                p.op("vector", lambda e: e.tensor_add(
                    lncol[:, 6:7], lncol[:, 6:7], lncol[:, 7:8]))
                tss = p.op("vector", lambda e: e.tensor_add(
                    lncol[:, 1:2], lncol[:, 4:5], lncol[:, 6:7]), sig=True)
                tm = p.op("vector", lambda e: e.tensor_scalar(
                    lncol[:, 2:3], lncol[:, 0:1], consts[:, C_I4096:C_I4096 + 1],
                    None, OP.mult), deps=[t_consts], sig=False)
                tv = p.op("vector", lambda e: e.tensor_scalar(
                    lncol[:, 1:2], lncol[:, 1:2], consts[:, C_I4096:C_I4096 + 1],
                    None, OP.mult))
                tv2 = p.op("vector", lambda e: e.scalar_tensor_tensor(
                    lncol[:, 3:4], lncol[:, 2:3], lncol[:, 2:3], lncol[:, 1:2],
                    OP.mult, OP.subtract), sig=True)
                tstd = p.op("scalar", lambda e: e.activation(
                    lncol[:, 3:4], lncol[:, 3:4], AF.Sqrt, scale=-1.0,
                    bias=consts[:, C_EPS:C_EPS + 1]), deps=[tv2], sig=True)
                trs = p.op("vector", lambda e: e.reciprocal(
                    lncol[:, 3:4], lncol[:, 3:4]), deps=[tstd], sig=True)
                txn = p.op("vector", lambda e, b=b: e.tensor_scalar(
                    xnb[:, :], b[:, :], lncol[:, 2:3], lncol[:, 3:4],
                    OP.subtract, OP.mult),
                    deps=[xn_w[t - 1]] if t >= 1 else [], sig=True)
                tw = p.dma("gpsimd", xn_tm[t * 128:(t + 1) * 128, :], xnb[:, :],
                           deps=[txn])
                xn_w.append(tw)
            for i in range(NI):
                tt = p.dma("sync", xnT[:, i * T:(i + 1) * T],
                           xn_tm[:, i * 128:(i + 1) * 128],
                           deps=xn_w if i == 0 else [], transpose=True)
                tok_xnT.append(tt)

        tok_A1_last = None
        tok_gdt = None
        tok_kv_done = []
        tok_wqf = []
        tok_xsum = None
        if "A1" in phases:
            wbuf = [sb(f"wbuf{i}", [128, 256], BF16, st=stack_a)
                    for i in range(4)]
            evf = sb("evf", [128, 1024], F32, st=stack_a)
            evf2 = sb("evf2", [128, 1024], F32, st=stack_a)
            rsc = sb("rsc", [128, 1024], F32, st=stack_a)
            evb = [sb(f"evb{i}", [128, 1024], BF16, st=stack_a)
                   for i in range(2)]
            cosb = sb("cosb", [128, T], BF16, st=stack_a)
            sinb = sb("sinb", [128, T], BF16, st=stack_a)
            t_cos = p.dma("gpsimd", cosb[:, :], cost_in)
            t_sin = p.dma("gpsimd", sinb[:, :], sint_in)

            xs_last = None
            for i in range(NI):
                xs_last = p.op("vector", lambda e, i=i: e.tensor_reduce(
                    xsum_c[:, i:i + 1], xnT[:, i * T:(i + 1) * T], AXX, OP.add),
                    deps=[tok_xnT[i]] if tok_xnT else [], sig=(i == NI - 1))
            tok_xsum = xs_last

            NOG = WA_COLS // 256
            mmtok = {}
            wcopy = {}
            bank_free = {s: None for s in range(6)}
            wgcast = []
            wccast = []
            spill_free = [None, None]
            evac_prev = None
            for og in range(NOG):
                c0 = og * 256
                subs = [og * 2 + s for s in range(2)]
                mm_last = [None, None]
                for i in range(NI):
                    wb = wbuf[i % 4]
                    if i >= 4:
                        pk = (og, i - 4)
                    elif og > 0:
                        pk = (og - 1, NI - 4 + i)
                    else:
                        pk = None
                    depw = [mmtok.get(pk), wcopy.get(pk)] if pk else []
                    tkw = p.dma("gpsimd", wb[:, :],
                                WA[i * 128:(i + 1) * 128, c0:c0 + 256], deps=depw)
                    last = None
                    nsubs = [s for s in range(2) if subs[s] != 53]
                    for s in nsubs:
                        deps_mm = [tkw]
                        if i == 0:
                            deps_mm += [bank_free[2 * s], bank_free[2 * s + 1]]
                            if tok_xnT:
                                deps_mm += tok_xnT[-8:]
                        islast = s == nsubs[-1]
                        tm0 = p.op("tensor", lambda e, wb=wb, s=s, i=i: e.matmul(
                            pb[2 * s][:, :], wb[:, s * 128:(s + 1) * 128],
                            xnT[:, i * T:i * T + 512],
                            start=(i == 0), stop=(i == NI - 1)),
                            deps=deps_mm, sig=(i == NI - 1))
                        tm1 = p.op("tensor", lambda e, wb=wb, s=s, i=i: e.matmul(
                            pb[2 * s + 1][:, :], wb[:, s * 128:(s + 1) * 128],
                            xnT[:, i * T + 512:(i + 1) * T],
                            start=(i == 0), stop=(i == NI - 1)),
                            sig=(i == NI - 1) or islast)
                        if i == NI - 1:
                            mm_last[s] = (tm0, tm1)
                        if islast:
                            last = tm1
                    mmtok[(og, i)] = last
                for s in range(2):
                    sub = subs[s]
                    if sub == 53:
                        continue
                    ev = _evac_osub(
                        p, sub, pb[2 * s], pb[2 * s + 1], mm_last[s],
                        evf, evf2, rsc, evb, spill_free, cosb, sinb,
                        t_cos, t_sin, kT, k_tm, v_tm, ident, pbt, ctx,
                        q_sp, hT_sp, g_fm, dt_fm, dtbias, acol, t_aneg,
                        evac_prev, pswapT=pswapT, t_psw=t_psw,
                        pb45=(pb[4], pb[5]))
                    bank_free[2 * s] = ev
                    bank_free[2 * s + 1] = ev
                    evac_prev = ev
            tok_A1_last = evac_prev

        stack_a.close()

        tok_out = []
        dramo = lambda n, s, d: nc.dram_tensor(n, s, d,
                                               kind="ExternalOutput").ap()
        if "DBG1" in phases:
            d_kT = dramo("d_kT", [128, KVH * T], BF16)
            d_ktm = dramo("d_ktm", [128, KVH * NCH * D], BF16)
            d_vtm = dramo("d_vtm", [128, KVH * NCH * D], BF16)
            d_dt = dramo("d_dt", [32, T], F32)
            d_g = dramo("d_g", [32, T], F32)
            d_xsum = dramo("d_xsum", [128, 32], F32)
            dl = [tok_A1_last, tok_xsum]
            tok_out.append(p.dma("sync", d_kT, kT[:, :], deps=dl))
            tok_out.append(p.dma("sync", d_ktm, k_tm[:, :], deps=dl))
            tok_out.append(p.dma("sync", d_vtm, v_tm[:, :], deps=dl))
            tok_out.append(p.dma("sync", d_dt, dt_fm[:, :], deps=dl))
            tok_out.append(p.dma("sync", d_g, g_fm[:, :], deps=dl))
            tok_out.append(p.dma("sync", d_xsum, xsum_c[:, :], deps=dl))

        tok_pay = []
        tok_b1 = None
        if "B1" in phases:
            stb = stack.enter_context(ExitStack())
            g_sc = sb("g_sc", [128, 32], F32, st=stb)
            gt_sc = sb("gt_sc", [32, 128], F32, st=stb)
            xcb = [sb(f"xcb{i}", [128, 128], BF16, st=stb) for i in range(2)]
            kdb = [sb(f"kdb{i}", [128, 128], BF16, st=stb) for i in range(2)]
            dep0 = [tok_A1_last, t_idf, t_tri]
            prev4 = None
            prev5 = None
            tok_gb = []
            for n in range(NCH):
                for kk in range(2):
                    k2 = n * 2 + kk
                    wgcast.append(p.dma(
                        "gpsimd", WG_bf[k2 * 256:(k2 + 1) * 256, :],
                        WG[k2 * 256:(k2 + 1) * 256, :]))
                t1 = p.op("tensor", lambda e, n=n: e.transpose(
                    pb[4][:, 0:32], g_fm[:, n * 128:(n + 1) * 128],
                    identf[0:32, 0:32]), deps=dep0 + [prev4], sig=True)
                t2 = p.op("vector", lambda e: e.tensor_copy(
                    g_sc[:, :], pb[4][:, 0:32]), deps=[t1], sig=True)
                t3 = p.op("tensor", lambda e, n=n: e.transpose(
                    pb[4][:, 32:64], dt_fm[:, n * 128:(n + 1) * 128],
                    identf[0:32, 0:32]), deps=[t2], sig=True)
                prev4 = p.op("vector", lambda e, n=n: e.tensor_copy(
                    dt_tm[:, n * 32:(n + 1) * 32], pb[4][:, 32:64]),
                    deps=[t3], sig=True)
                m1 = p.op("tensor", lambda e: e.matmul(
                    pb[5][:, 0:32], tri[:, :], g_sc[:, :],
                    start=True, stop=True), deps=[t2, prev5], sig=True)
                gv = p.op("vector", lambda e, n=n: e.tensor_copy(
                    G_tm[:, n * 32:(n + 1) * 32], pb[5][:, 0:32]),
                    deps=[m1], sig=True)
                m2 = p.op("tensor", lambda e: e.matmul(
                    pb[5][0:32, 64:192], g_sc[:, :], tri[:, :],
                    start=True, stop=True), deps=[gv], sig=True)
                gtv = p.op("vector", lambda e: e.tensor_copy(
                    gt_sc[:, :], pb[5][0:32, 64:192]),
                    deps=[m2] + tok_gb[-1:], sig=True)
                p.op("scalar", lambda e, n=n: e.activation(
                    expG_tm[:, n * 32:(n + 1) * 32],
                    G_tm[:, n * 32:(n + 1) * 32], AF.Exp), deps=[gv], sig=True)
                m3 = p.op("tensor", lambda e: e.matmul(
                    pb[5][0:1, 192:224], gt_sc[:, 127:128], identf[0:32, 0:32],
                    start=True, stop=True), deps=[gtv], sig=True)
                cs_r = p.op("vector", lambda e: e.tensor_copy(
                    rows[0:1, 544:576], pb[5][0:1, 192:224]),
                    deps=[m3], sig=True)
                p.op("scalar", lambda e, n=n: e.activation(
                    rows[0:1, n * 32:(n + 1) * 32], rows[0:1, 544:576],
                    AF.Exp), deps=[cs_r], sig=True)
                m4 = p.op("tensor", lambda e: e.matmul(
                    pb[5][:, 224:256], ones_r[0:1, 0:128], rows[0:1, 544:576],
                    start=True, stop=True), deps=[cs_r], sig=True)
                sb1 = p.op("vector", lambda e, n=n: e.tensor_sub(
                    edec_tm[:, n * 32:(n + 1) * 32], pb[5][:, 224:256],
                    G_tm[:, n * 32:(n + 1) * 32]), deps=[m4], sig=True)
                p.op("scalar", lambda e, n=n: e.activation(
                    edec_tm[:, n * 32:(n + 1) * 32],
                    edec_tm[:, n * 32:(n + 1) * 32], AF.Exp),
                    deps=[sb1], sig=True)
                prev5 = sb1
                tok_gb.append(p.dma(
                    "sync",
                    g_bounce[n:n + 1, :].rearrange("a (h i) -> (a h) i", h=32),
                    gt_sc[:, :], deps=[gtv]))
            p.op("vector", lambda e: e.memset(rows[0:1, 256 + 224:256 + 256],
                                              1.0))
            for n in range(6, -1, -1):
                p.op("vector", lambda e, n=n: e.tensor_mul(
                    rows[0:1, 256 + n * 32:256 + (n + 1) * 32],
                    rows[0:1, 256 + (n + 1) * 32:256 + (n + 2) * 32],
                    rows[0:1, (n + 1) * 32:(n + 2) * 32]))
            tP = p.op("vector", lambda e: e.tensor_mul(
                rows[0:1, 512:544], rows[0:1, 256:288], rows[0:1, 0:32]),
                sig=True)
            prevb = tP
            for n in range(NCH):
                mm_d = p.op("tensor", lambda e, n=n: e.matmul(
                    pb[5][:, 256:288], ones_r[0:1, 0:128],
                    rows[0:1, n * 32:(n + 1) * 32], start=True, stop=True),
                    deps=[prevb], sig=True)
                ev_d = p.op("vector", lambda e, n=n: e.tensor_copy(
                    dec_bc[:, n * 32:(n + 1) * 32], pb[5][:, 256:288]),
                    deps=[mm_d], sig=True)
                mm_f = p.op("tensor", lambda e, n=n: e.matmul(
                    pb[5][:, 288:320], ones_r[0:1, 0:128],
                    rows[0:1, 256 + n * 32:256 + (n + 1) * 32],
                    start=True, stop=True), deps=[ev_d], sig=True)
                prevb = p.op("vector", lambda e, n=n: e.tensor_copy(
                    fut_bc[:, n * 32:(n + 1) * 32], pb[5][:, 288:320]),
                    deps=[mm_f], sig=True)
            sprev = {0: None, 1: None}
            mmS = {}
            for h in range(H):
                kv = h // 4
                pk = h % 2
                mlast = None
                for n in range(NCH):
                    bi = n % 2
                    war = [mmS.get((h, n - 2)), mmS.get((h - 1, NCH - 2 + n))]
                    tx = p.op("vector", lambda e, kv=kv, n=n, h=h, bi=bi:
                              e.tensor_scalar(
                                  xcb[bi][:, :],
                                  v_tm[:, (kv * 8 + n) * 128:(kv * 8 + n + 1) * 128],
                                  dt_tm[:, n * 32 + h:n * 32 + h + 1],
                                  None, OP.mult),
                              deps=[d for d in war if d] + [prevb], sig=True)
                    tk = p.op("vector", lambda e, kv=kv, n=n, h=h, bi=bi:
                              e.tensor_scalar(
                                  kdb[bi][:, :],
                                  k_tm[:, (kv * 8 + n) * 128:(kv * 8 + n + 1) * 128],
                                  edec_tm[:, n * 32 + h:n * 32 + h + 1],
                                  fut_bc[:, n * 32 + h:n * 32 + h + 1],
                                  OP.mult, OP.mult), sig=True)
                    mlast = p.op("tensor", lambda e, h=h, n=n, bi=bi, pk=pk:
                                 e.matmul(pb[pk][:, 0:128], kdb[bi][:, :],
                                          xcb[bi][:, :], start=(n == 0),
                                          stop=(n == NCH - 1)),
                                 deps=[tx, tk] +
                                 ([sprev[pk]] if n == 0 else []), sig=True)
                    mmS[(h, n)] = mlast
                sprev[pk] = p.op("vector", lambda e, h=h, pk=pk: e.tensor_copy(
                    state[:, h * 128:(h + 1) * 128], pb[pk][:, 0:128]),
                    deps=[mlast], sig=True)
            tok_b1 = sprev[1]
            tok_pay.append(p.dma("sync", pay[:, 0:4096], state[:, :],
                                 deps=[sprev[0], sprev[1]]))
            tok_pay.append(p.dma("sync", pay[:, PAY_XS:PAY_XS + 32],
                                 xsum_c[:, :], deps=[tok_xsum]))
            tok_pay.append(p.dma("sync", pay[0:1, PAY_P:PAY_P + 32],
                                 rows[0:1, 512:544], deps=[tP]))
            stb.close()
            stack_r2.close()

        if "DBGB1" in phases:
            d_state = dramo("d_state", [128, H * D], F32)
            d_rows = dramo("d_rows", [1, 576], F32)
            d_Gtm = dramo("d_Gtm", [128, NCH * 32], F32)
            d_edec = dramo("d_edec", [128, NCH * 32], F32)
            d_decbc = dramo("d_decbc", [128, NCH * 32], F32)
            tok_out.append(p.dma("sync", d_state, state[:, :], deps=[tok_b1]))
            tok_out.append(p.dma("sync", d_rows, rows[:, :], deps=[tok_b1]))
            tok_out.append(p.dma("sync", d_Gtm, G_tm[:, :], deps=[tok_b1]))
            tok_out.append(p.dma("sync", d_edec, edec_tm[:, :], deps=[tok_b1]))
            tok_out.append(p.dma("sync", d_decbc, dec_bc[:, :], deps=[tok_b1]))

        tok_cc = None
        tcc = None
        if "CC" in phases:
            tcc = p.op("gpsimd", lambda e: e.collective_compute(
                "AllGather", OP.bypass,
                replica_groups=[list(range(NCORE))],
                ins=[pay.opt()], outs=[gath.opt()]), deps=tok_pay, sig=True)

        tok_c1a = None
        if "C1a" in phases:
            stc = stack.enter_context(ExitStack())
            wbuf2 = [sb(f"wbufc{i}", [128, 256], BF16, st=stc) for i in range(4)]
            gsb = [sb(f"gsb{i}", [128, 1024], BF16, st=stc) for i in range(2)]
            ogs = [(og * 2, og * 256, 2) for og in range(16)]
            mmtok2 = {}
            bank_free2 = {s: tok_b1 for s in range(6)}
            gs_free = [None, None]
            guard0 = [tok_A1_last, tok_b1] + p.qtoks("gpsimd")
            ev2 = None
            for ogi, (s0, c0, ns) in enumerate(ogs):
                if ogi < 16:
                    wccast.append(p.dma(
                        "gpsimd", WC_bf[ogi * 256:(ogi + 1) * 256, :],
                        WC[ogi * 256:(ogi + 1) * 256, :]))
                mm_last2 = {}
                for i in range(NI):
                    wb = wbuf2[i % 4]
                    if i >= 4:
                        pkk = (ogi, i - 4)
                    elif ogi > 0:
                        pkk = (ogi - 1, NI - 4 + i)
                    else:
                        pkk = None
                    depw = [mmtok2.get(pkk)] if pkk else list(guard0)
                    depw = depw + [wgcast[i // 2]]
                    tkw = p.dma("sync" if i % 2 == 0 else "scalar",
                                wb[:, 0:128 * ns],
                                WG_bf[i * 128:(i + 1) * 128, c0:c0 + 128 * ns],
                                deps=depw)
                    last = None
                    for s in range(ns):
                        deps_mm = [tkw]
                        if i == 0:
                            deps_mm += [bank_free2[2 * s], bank_free2[2 * s + 1]]
                        tm0 = p.op("tensor", lambda e, wb=wb, s=s, i=i: e.matmul(
                            pb[2 * s][:, :], wb[:, s * 128:(s + 1) * 128],
                            xnT[:, i * T:i * T + 512],
                            start=(i == 0), stop=(i == NI - 1)),
                            deps=deps_mm, sig=(i == NI - 1))
                        tm1 = p.op("tensor", lambda e, wb=wb, s=s, i=i: e.matmul(
                            pb[2 * s + 1][:, :], wb[:, s * 128:(s + 1) * 128],
                            xnT[:, i * T + 512:(i + 1) * T],
                            start=(i == 0), stop=(i == NI - 1)),
                            sig=(i == NI - 1) or (s == ns - 1))
                        if i == NI - 1:
                            mm_last2[s] = (tm0, tm1)
                        if s == ns - 1:
                            last = tm1
                    mmtok2[(ogi, i)] = last
                for s in range(ns):
                    sub = s0 + s
                    tm0, tm1 = mm_last2[s]
                    gb = gsb[sub % 2]
                    a0 = p.op("scalar", lambda e, gb=gb, s=s: e.activation(
                        gb[:, 0:512], pb[2 * s][:, :], AF.Silu),
                        deps=[tm0, gs_free[sub % 2]], sig=True)
                    a1 = p.op("scalar", lambda e, gb=gb, s=s: e.activation(
                        gb[:, 512:1024], pb[2 * s + 1][:, :], AF.Silu),
                        deps=[tm1], sig=True)
                    tw = p.dma("gpsimd", gs_sp[sub * 128:(sub + 1) * 128, :],
                               gb[:, :], deps=[a1])
                    gs_free[sub % 2] = tw
                    bank_free2[2 * s] = a1
                    bank_free2[2 * s + 1] = a1
                    ev2 = a1
            tok_c1a = ev2
            c1a_done = [tok_c1a, mmtok2[(len(ogs) - 1, NI - 1)]] + \
                p.qtoks("gpsimd")
            stc.close()
        else:
            c1a_done = [tok_b1]
        if tcc is not None:
            tok_cc = p.op("gpsimd", lambda e: e.nop(), deps=[tcc], sig=True)
        stack_xnT.close()

        tok_pre = None
        tok_upq = []
        if "B2" in phases:
            stp = stack.enter_context(ExitStack())
            gstage = [sb(f"gstage{i}", [128, HID // 2], F32, st=stp) for i in range(2)]
            paux = sb("paux", [1, 1024], F32, st=stp)
            cmf_s = sb("cmf_s", [1, 256], F32, st=stp)
            omcmf_s = sb("omcmf_s", [1, 256], F32, st=stp)
            wbc = sb("wbc", [128, 256], F32, st=stp)
            keys_s = sb("keys_s", [128, 8], F32, st=stp)
            sel_s = sb("sel_s", [8, 512], F32, st=stp)
            wqfb = [sb(f"wqfb{i}", [128, 128], BF16, st=stp) for i in range(2)]
            xst = sb("xst", [128, 32], F32, st=stp)
            xsum_bf = sb("xsum_bf", [128, 32], BF16, st=stp)
            ce_cols = sb("ce_cols", [128, 4], F32, st=stp)
            upb = [sb(f"upb{i}", [128, 2048], BF16, st=stp) for i in range(2)]
            qrow = sb("qrow", [1, 128], F32, st=stp)
            qcol = sb("qcol", [128, 1], F32, st=stp)
            ccol = sb("ccol", [8, 1], F32, st=stp)

            p.dma("sync", cmf_s[:, :], cmf_in, deps=[tok_cc] + c1a_done)
            p.dma("sync", omcmf_s[:, :], omcmf_in)
            p.dma("sync", keys_s[:, :], keysT_in)
            t_sel = p.dma("sync", sel_s[:, :], sel_in)
            tpr = []
            for j in range(NCORE):
                tpr.append(p.dma(
                    "sync", paux[0:1, j * 32:(j + 1) * 32],
                    gath[j * 128:j * 128 + 1, PAY_P:PAY_P + 32],
                    deps=[tok_cc] + c1a_done))
            p.op("vector", lambda e: e.tensor_mul(
                paux[0:1, 256:512], paux[0:1, 0:256], cmf_s[0:1, :]),
                deps=tpr + [tok_cc])
            p.op("vector", lambda e: e.tensor_add(
                paux[0:1, 256:512], paux[0:1, 256:512], omcmf_s[0:1, :]))
            p.op("vector", lambda e: e.memset(paux[0:1, 768:800], 1.0))
            for j in range(7, -1, -1):
                p.op("vector", lambda e, j=j: e.tensor_mul(
                    paux[0:1, 512 + j * 32:512 + (j + 1) * 32],
                    cmf_s[0:1, j * 32:(j + 1) * 32], paux[0:1, 768:800]))
                p.op("vector", lambda e, j=j: e.tensor_mul(
                    paux[0:1, 768:800], paux[0:1, 768:800],
                    paux[0:1, 256 + j * 32:256 + (j + 1) * 32]))
            tw_rows = p.op("vector", lambda e: e.nop(), sig=True)
            prevw = tw_rows
            for j in range(NCORE):
                mmw = p.op("tensor", lambda e, j=j: e.matmul(
                    pb[5][:, 320:352], ones_r[0:1, 0:128],
                    paux[0:1, 512 + j * 32:512 + (j + 1) * 32],
                    start=True, stop=True), deps=[prevw], sig=True)
                prevw = p.op("vector", lambda e, j=j: e.tensor_copy(
                    wbc[:, j * 32:(j + 1) * 32], pb[5][:, 320:352]),
                    deps=[mmw], sig=True)
            tz = p.op("vector", lambda e: e.memset(state[:, :], 0.0),
                      deps=[tok_pay[0] if tok_pay else None, tok_cc] +
                      c1a_done, sig=True)
            prevg = tz
            ghalf_done = {0: None, 1: None}
            for j in range(NCORE):
                for gh in range(2):
                    gs = gstage[gh]
                    tg = p.dma("sync", gs[:, :],
                               gath[j * 128:(j + 1) * 128,
                                    gh * 2048:(gh + 1) * 2048],
                               deps=[ghalf_done[gh], tok_cc, tz])
                    for hq in range(H // 2):
                        h = gh * 16 + hq
                        prevg = p.op("vector", lambda e, j=j, h=h, hq=hq,
                                     gs=gs: e.scalar_tensor_tensor(
                                         state[:, h * 128:(h + 1) * 128],
                                         gs[:, hq * 128:(hq + 1) * 128],
                                         wbc[:, j * 32 + h:j * 32 + h + 1],
                                         state[:, h * 128:(h + 1) * 128],
                                         OP.mult, OP.add),
                                     deps=[tg, prevw], par=8,
                                     sig=(hq == H // 2 - 1))
                    ghalf_done[gh] = prevg
            tok_in = prevg
            prevx = None
            for j in range(NCORE):
                tg = p.dma("sync", xst[:, :],
                           gath[j * 128:(j + 1) * 128, PAY_XS:PAY_XS + 32],
                           deps=[tok_cc, prevx, tok_pay[1] if len(tok_pay) > 1
                                 else None])
                if j == 0:
                    prevx = p.op("vector", lambda e: e.tensor_copy(
                        xsum_c[:, :], xst[:, :]), deps=[tg], sig=True)
                else:
                    prevx = p.op("vector", lambda e: e.tensor_add(
                        xsum_c[:, :], xsum_c[:, :], xst[:, :]),
                        deps=[tg], sig=True)
            txb = p.op("vector", lambda e: e.tensor_copy(
                xsum_bf[:, :], xsum_c[:, :]), deps=[prevx], sig=True)
            mmq = {}
            for i in range(NI):
                tkw = p.dma("gpsimd", wqfb[i % 2][:, :],
                            WA[i * 128:(i + 1) * 128, 6784:6912],
                            deps=[mmq.get(i - 2)] +
                            (c1a_done if i < 2 else []))
                mmq[i] = p.op("tensor", lambda e, i=i: e.matmul(
                    pb[4][0:1, 0:128], xsum_bf[:, i:i + 1], wqfb[i % 2][:, :],
                    start=(i == 0), stop=(i == NI - 1)),
                    deps=[tkw, txb], sig=True)
            tqr = p.op("vector", lambda e: e.tensor_scalar(
                qrow[0:1, :], pb[4][0:1, 0:128], consts[0:1, C_QS:C_QS + 1],
                None, OP.mult), deps=[mmq[NI - 1]], sig=True)
            tb1 = p.dma("sync", r_bounce[0:1, 0:128], qrow[0:1, :], deps=[tqr])
            tb2 = p.dma("sync", qcol[:, :],
                        r_bounce[0:1, 0:128].rearrange("a k -> k a"),
                        deps=[tb1])
            mms = p.op("tensor", lambda e: e.matmul(
                pb[4][0:1, 128:136], qcol[:, :], keys_s[:, :],
                start=True, stop=True), deps=[tb2], sig=True)
            p.op("vector", lambda e: e.tensor_copy(
                paux[0:1, 800:808], pb[4][0:1, 128:136]), deps=[mms])
            p.op("vector", lambda e: e.tensor_reduce(
                paux[0:1, 824:825], paux[0:1, 800:808], AXX, OP.max))
            p.op("vector", lambda e: e.tensor_scalar(
                paux[0:1, 800:808], paux[0:1, 800:808],
                paux[0:1, 824:825], None, OP.subtract))
            tso = p.op("scalar", lambda e: e.activation(
                paux[0:1, 800:808], paux[0:1, 800:808], AF.Exp),
                deps=[(p.msem["vector"], p.mcount["vector"])], sig=True)
            p.op("vector", lambda e: e.tensor_reduce(
                paux[0:1, 824:825], paux[0:1, 800:808], AXX, OP.add),
                deps=[tso])
            p.op("vector", lambda e: e.reciprocal(
                paux[0:1, 825:826], paux[0:1, 824:825]))
            tcoef = p.op("vector", lambda e: e.tensor_scalar(
                paux[0:1, 808:816], paux[0:1, 800:808],
                paux[0:1, 825:826], None, OP.mult), sig=True)
            tb3 = p.dma("sync", r_bounce[1:2, 0:8], paux[0:1, 808:816],
                        deps=[tcoef])
            tb4 = p.dma("sync", ccol[:, :],
                        r_bounce[1:2, 0:8].rearrange("a k -> k a"),
                        deps=[tb3])
            prevc = None
            for rr in range(4):
                mmc = p.op("tensor", lambda e, rr=rr: e.matmul(
                    pb[4][:, 256:257], sel_s[:, rr * 128:(rr + 1) * 128],
                    ccol[:, :], start=True, stop=True),
                    deps=[tb4, t_sel, prevc], sig=True)
                prevc = p.op("vector", lambda e, rr=rr: e.tensor_copy(
                    ce_cols[:, rr:rr + 1], pb[4][:, 256:257]),
                    deps=[mmc], sig=True)
            uprev = {0: None, 1: None}
            for ui, (rb, hf) in enumerate([(r, h) for r in range(4)
                                           for h in range(2)]):
                ub = upb[ui % 2]
                tu = p.dma("gpsimd", ub[:, :],
                           UPT_in[rb * 128:(rb + 1) * 128,
                                  hf * 2048:(hf + 1) * 2048],
                           deps=[uprev[ui % 2]] + (c1a_done if ui < 2 else []))
                tsc = p.op("vector", lambda e, rb=rb, ub=ub: e.tensor_scalar(
                    ub[:, :], ub[:, :], ce_cols[:, rb:rb + 1],
                    None, OP.mult), deps=[tu, prevc], sig=True)
                tws = p.dma("gpsimd", upq[rb * 128:(rb + 1) * 128,
                                          hf * 2048:(hf + 1) * 2048],
                            ub[:, :], deps=[tsc])
                tok_upq.append(tws)
                uprev[ui % 2] = tws
            tok_pre = tcoef
            if "DBGPRE" in phases:
                d_in = dramo("d_in", [128, H * D], F32)
                d_coef = dramo("d_coef", [1, 1024], F32)
                d_ce = dramo("d_ce", [128, 4], F32)
                tok_out.append(p.dma("sync", d_in, state[:, :], deps=[tok_in]))
                tok_out.append(p.dma("sync", d_coef, paux[:, :], deps=[tcoef]))
                tok_out.append(p.dma("sync", d_ce, ce_cols[:, :], deps=[prevc]))
            stp.close()

        tok_b2 = None
        yT = None
        if "B2" in phases:
            yT = sb("yT", [128, H * T], BF16)
            stb2 = stack.enter_context(ExitStack())
            q_half = sb("q_half", [128, 16 * T], BF16, st=stb2)
            grow = sb("grow", [1, 2048], F32, st=stb2)
            dsb = sb("dsb", [128, 2048], F32, st=stb2)
            decb = sb("decb", [128, 2048], BF16, st=stb2)
            xcb4 = [sb(f"xcb4{i}", [128, 512], BF16, st=stb2) for i in range(2)]
            kdb4 = [sb(f"kdb4{i}", [128, 512], BF16, st=stb2) for i in range(2)]
            sdec4 = [sb(f"sdec4{i}", [128, 512], BF16, st=stb2)
                     for i in range(2)]
            pvb4 = [sb(f"pvb4{i}", [128, 512], BF16, st=stb2) for i in range(2)]
            rcol = sb("rcol", [128, 16], F32, st=stb2)
            yscb = [sb(f"yscb{i}", [128, 512], BF16, st=stb2)
                    for i in range(2)]
            yfv = [dsb[:, 0:512], dsb[:, 512:1024]]
            ysbv = [dsb[:, 1024:1536], dsb[:, 1536:2048]]

            def expand(ap_cols, reps=128):
                return ap_cols.unsqueeze(2).broadcast_to(
                    (128, ap_cols.shape[1], reps))

            def expand_tile(ap_tile, k=4):
                return ap_tile.unsqueeze(1).broadcast_to((128, k, 128))

            last_q_use = None
            pb_free = {0: tok_c1a, 1: tok_c1a, 2: tok_c1a, 3: tok_c1a,
                       4: prevc, 5: prevc}
            pbt_free2 = {0: ctx["pbt_free"][0], 1: ctx["pbt_free"][1]}
            dep_q0 = p.qtoks("gpsimd") + [tok_pre]
            prev_tys = None
            prev_tsq = None
            prev_mmD3 = None
            xckd_free = {0: None, 1: None}
            sdec_free = {0: None, 1: None}
            pvb_free = {0: None, 1: None}
            yscb_free = {0: None, 1: None}
            for hh in range(2):
                qtk = []
                for l in range(16):
                    qtk.append(p.dma(
                        "sync", q_half[:, l * T:(l + 1) * T],
                        q_sp[(hh * 16 + l) * 128:(hh * 16 + l + 1) * 128, :],
                        deps=dep_q0 + ([last_q_use] if last_q_use else [])))
                for n in range(NCH):
                    hg0 = hh * 16
                    tgr = p.dma("sync", grow[0:1, :],
                                g_bounce[n:n + 1, hh * 2048:(hh + 1) * 2048],
                                deps=(tok_gb if (hh == 0 and n == 0) else []) +
                                ([prev_mmD3] if prev_mmD3 else []))
                    mmD = []
                    for qd in range(4):
                        mmD.append(p.op("tensor", lambda e, qd=qd: e.matmul(
                            pb[qd][:, :], ones_r[0:1, 0:128],
                            grow[0:1, qd * 512:(qd + 1) * 512],
                            start=True, stop=True),
                            deps=[tgr, pb_free[qd]], sig=True))
                    prev_mmD3 = mmD[3]
                    trl = []
                    for l in range(16):
                        hg = hg0 + l
                        dprev = ([prev_tys, prev_tsq]
                                 if (l == 0 and prev_tys) else [])
                        trl.append(p.op("scalar", lambda e, l=l, hg=hg, n=n:
                                        e.activation(
                                            dsb[:, l * 128:(l + 1) * 128],
                                            pb[l // 4][:, (l % 4) * 128:
                                                       (l % 4 + 1) * 128],
                                            AF.Relu, scale=-1.0,
                                            bias=G_tm[:, n * 32 + hg:
                                                      n * 32 + hg + 1]),
                                        deps=[mmD[l // 4]] + dprev,
                                        par=(1 if l == 0 else 4), sig=True))
                    for qd in range(4):
                        pb_free[qd] = trl[qd * 4 + 3]
                    tex = p.op("scalar", lambda e: e.activation(
                        decb[:, :], dsb[:, :], AF.Exp, scale=-1.0),
                        deps=[trl[-1]], sig=True)
                    tmask = p.op("vector", lambda e: e.tensor_tensor(
                        decb[:, :].rearrange("p (a b) -> p a b", a=16),
                        decb[:, :].rearrange("p (a b) -> p a b", a=16),
                        expand_tile(tri_bf[:, :], 16), OP.mult),
                        deps=[tex, t_tribf], sig=True)
                    for kvpair in range(2):
                        st2 = {}
                        for pi in (0, 1):
                            kvl = 2 * kvpair + pi
                            kv = hh * 4 + kvl
                            hgk = hg0 + 4 * kvl
                            st2[pi] = dict(kvl=kvl, kv=kv, hgk=hgk,
                                           pS=pb[4 + pi], pY1=pb[0 + pi],
                                           pY2=pb[2 + pi])
                        for pi in (0, 1):
                            s = st2[pi]
                            mm = []
                            for m in range(4):
                                mm.append(p.op(
                                    "tensor", lambda e, s=s, n=n, m=m, pi=pi:
                                    e.matmul(
                                        s["pS"][:, m * 128:(m + 1) * 128],
                                        kT[:, s["kv"] * T + n * 128:
                                           s["kv"] * T + (n + 1) * 128],
                                        q_half[:, (4 * s["kvl"] + m) * T +
                                               n * 128:
                                               (4 * s["kvl"] + m) * T +
                                               (n + 1) * 128],
                                        start=True, stop=True),
                                    deps=[pb_free[4 + pi]] + qtk,
                                    sig=(m == 3)))
                            s["mmsc"] = mm[3]
                        for pi in (0, 1):
                            s = st2[pi]
                            s["txc"] = p.op("gpsimd", lambda e, s=s, n=n, pi=pi:
                                            e.tensor_tensor(
                                                xcb4[pi][:, :].rearrange(
                                                    "p (a b) -> p a b", a=4),
                                                expand_tile(v_tm[:, (s["kv"] * 8 + n) * 128:
                                                                 (s["kv"] * 8 + n + 1) * 128]),
                                                expand(dt_tm[:, n * 32 + s["hgk"]:
                                                             n * 32 + s["hgk"] + 4]),
                                                OP.mult),
                                            deps=[xckd_free[pi]], par=2,
                                            sig=True)
                        for pi in (0, 1):
                            s = st2[pi]
                            s["tkd"] = p.op("gpsimd", lambda e, s=s, n=n, pi=pi:
                                            e.tensor_tensor(
                                                kdb4[pi][:, :].rearrange(
                                                    "p (a b) -> p a b", a=4),
                                                expand_tile(k_tm[:, (s["kv"] * 8 + n) * 128:
                                                                 (s["kv"] * 8 + n + 1) * 128]),
                                                expand(edec_tm[:, n * 32 + s["hgk"]:
                                                               n * 32 + s["hgk"] + 4]),
                                                OP.mult), par=2, sig=True)
                        for pi in (0, 1):
                            s = st2[pi]
                            s["tsd"] = p.op("vector", lambda e, s=s, pi=pi:
                                            e.tensor_tensor(
                                                sdec4[pi][:, :], s["pS"][:, :],
                                                decb[:, s["kvl"] * 512:
                                                     (s["kvl"] + 1) * 512],
                                                OP.mult),
                                            deps=[s["mmsc"], tmask,
                                                  sdec_free[pi]],
                                            par=2, sig=True)
                        for pi in (0, 1):
                            s = st2[pi]
                            s["tpv"] = p.op("vector", lambda e, s=s, pi=pi:
                                            e.tensor_copy(
                                                pvb4[pi][:, :],
                                                state[:, s["hgk"] * 128:
                                                      (s["hgk"] + 4) * 128]),
                                            deps=[pvb_free[pi]], par=2,
                                            sig=True)
                        for pi in (0, 1):
                            s = st2[pi]
                            mm = []
                            for m in range(4):
                                mm.append(p.op(
                                    "tensor", lambda e, s=s, n=n, m=m, pi=pi:
                                    e.matmul(
                                        s["pY1"][:, m * 128:(m + 1) * 128],
                                        q_half[:, (4 * s["kvl"] + m) * T +
                                               n * 128:
                                               (4 * s["kvl"] + m) * T +
                                               (n + 1) * 128],
                                        pvb4[pi][:, m * 128:(m + 1) * 128],
                                        start=True, stop=True),
                                    deps=[s["tpv"], pb_free[0 + pi]],
                                    sig=(m == 3)))
                            s["mmy1"] = mm[3]
                        for pi in (0, 1):
                            s = st2[pi]
                            mm = []
                            for m in range(4):
                                mm.append(p.op(
                                    "tensor", lambda e, s=s, m=m, pi=pi: e.matmul(
                                        s["pY2"][:, m * 128:(m + 1) * 128],
                                        sdec4[pi][:, m * 128:(m + 1) * 128],
                                        xcb4[pi][:, m * 128:(m + 1) * 128],
                                        start=True, stop=True),
                                    deps=[s["tsd"], s["txc"],
                                          pb_free[2 + pi]], sig=(m == 3)))
                            s["mmy2"] = mm[3]
                        for pi in (0, 1):
                            s = st2[pi]
                            s["ty1"] = p.op("vector", lambda e, s=s, n=n, pi=pi:
                                            e.tensor_tensor(
                                                ysbv[pi].rearrange(
                                                    "p (a b) -> p a b", a=4),
                                                s["pY1"][:, :].rearrange(
                                                    "p (a b) -> p a b", a=4),
                                                expand(expG_tm[:, n * 32 + s["hgk"]:
                                                               n * 32 + s["hgk"] + 4]),
                                                OP.mult),
                                            deps=[s["mmy1"]], par=2, sig=True)
                        for pi in (0, 1):
                            s = st2[pi]
                            s["ty2"] = p.op("vector", lambda e, s=s, pi=pi:
                                            e.tensor_add(
                                                yfv[pi], ysbv[pi],
                                                s["pY2"][:, :]),
                                            deps=[s["mmy2"]], par=2, sig=True)
                            pb_free[0 + pi] = s["ty1"]
                            pb_free[2 + pi] = s["ty2"]
                        for pi in (0, 1):
                            s = st2[pi]
                            mm = []
                            for m in range(4):
                                mm.append(p.op(
                                    "tensor", lambda e, s=s, m=m, pi=pi: e.matmul(
                                        s["pS"][:, m * 128:(m + 1) * 128],
                                        kdb4[pi][:, m * 128:(m + 1) * 128],
                                        xcb4[pi][:, m * 128:(m + 1) * 128],
                                        start=True, stop=True),
                                    deps=[s["tsd"], s["tkd"]], sig=(m == 3)))
                            s["mmcs"] = mm[3]
                            sdec_free[pi] = s["mmy2"]
                            xckd_free[pi] = s["mmcs"]
                        for pi in (0, 1):
                            s = st2[pi]
                            tst = None
                            for m in range(4):
                                tst = p.op("vector", lambda e, s=s, n=n, m=m, pi=pi:
                                           e.scalar_tensor_tensor(
                                               state[:, (s["hgk"] + m) * 128:
                                                     (s["hgk"] + m + 1) * 128],
                                               state[:, (s["hgk"] + m) * 128:
                                                     (s["hgk"] + m + 1) * 128],
                                               dec_bc[:, n * 32 + s["hgk"] + m:
                                                      n * 32 + s["hgk"] + m + 1],
                                               s["pS"][:, m * 128:(m + 1) * 128],
                                               OP.mult, OP.add),
                                           deps=[s["mmcs"]], par=2,
                                           sig=(m == 3))
                            pb_free[4 + pi] = tst
                            pvb_free[pi] = s["mmy1"]
                        for pi in (0, 1):
                            s = st2[pi]
                            s["tsq"] = p.op("scalar", lambda e, s=s, pi=pi:
                                            e.activation(ysbv[pi], yfv[pi],
                                                         AF.Square),
                                            deps=[s["ty2"]], par=2, sig=True)
                        for pi in (0, 1):
                            s = st2[pi]
                            trd = None
                            for m in range(4):
                                trd = p.op("vector", lambda e, s=s, m=m, pi=pi:
                                           e.tensor_reduce(
                                               rcol[:, 8 * pi + m:
                                                    8 * pi + m + 1],
                                               ysbv[pi][:, m * 128:
                                                        (m + 1) * 128],
                                               AXX, OP.add),
                                           deps=[s["tsq"]], par=2,
                                           sig=(m == 3))
                            s["trd"] = trd
                        for pi in (0, 1):
                            s = st2[pi]
                            s["tms"] = p.op("vector", lambda e, s=s, pi=pi:
                                            e.tensor_scalar(
                                                rcol[:, 8 * pi:8 * pi + 4],
                                                rcol[:, 8 * pi:8 * pi + 4],
                                                consts[:, C_I128:C_I128 + 1],
                                                None, OP.mult),
                                            deps=[s["trd"]], par=2, sig=True)
                        for pi in (0, 1):
                            s = st2[pi]
                            s["tsr"] = p.op("scalar", lambda e, s=s, pi=pi:
                                            e.activation(
                                                rcol[:, 8 * pi + 4:8 * pi + 8],
                                                rcol[:, 8 * pi:8 * pi + 4],
                                                AF.Sqrt,
                                                bias=consts[:, C_EPS:C_EPS + 1]),
                                            deps=[s["tms"]], par=2, sig=True)
                        for pi in (0, 1):
                            s = st2[pi]
                            s["trc"] = p.op("vector", lambda e, s=s, pi=pi:
                                            e.reciprocal(
                                                rcol[:, 8 * pi:8 * pi + 4],
                                                rcol[:, 8 * pi + 4:8 * pi + 8]),
                                            deps=[s["tsr"]], par=2, sig=True)
                        for pi in (0, 1):
                            s = st2[pi]
                            s["tys"] = p.op("vector", lambda e, s=s, pi=pi:
                                            e.tensor_tensor(
                                                yscb[pi][:, :].rearrange(
                                                    "p (a b) -> p a b", a=4),
                                                yfv[pi].rearrange(
                                                    "p (a b) -> p a b", a=4),
                                                expand(rcol[:, 8 * pi:
                                                            8 * pi + 4]),
                                                OP.mult),
                                            deps=[s["trc"], yscb_free[pi]],
                                            par=2, sig=True)
                            prev_tys = s["tys"]
                            prev_tsq = s["tsq"]
                        for pi in (0, 1):
                            s = st2[pi]
                            mm = []
                            for m in range(4):
                                mm.append(p.op(
                                    "tensor", lambda e, s=s, m=m, pi=pi:
                                    e.transpose(
                                        pbt[pi][:, m * 128:(m + 1) * 128],
                                        yscb[pi][:, m * 128:(m + 1) * 128],
                                        ident[:, :]),
                                    deps=[s["tys"], pbt_free2[pi], t_id],
                                    sig=(m == 3)))
                            s["mmt"] = mm[3]
                        for pi in (0, 1):
                            s = st2[pi]
                            tyev = p.op("scalar", lambda e, s=s, n=n, pi=pi:
                                        e.copy(
                                            yT[:, :].rearrange(
                                                "p (h t) -> p h t", h=32)[
                                                :, s["hgk"]:s["hgk"] + 4,
                                                n * 128:(n + 1) * 128],
                                            pbt[pi][:, 0:512].rearrange(
                                                "p (a b) -> p a b", a=4)),
                                        deps=[s["mmt"]], par=2, sig=True)
                            pbt_free2[pi] = tyev
                            yscb_free[pi] = s["mmt"]
                            last_q_use = s["mmy1"]
            tok_b2 = last_q_use
            if "DBGB2" in phases:
                d_yT = dramo("d_yT", [128, H * T], BF16)
                tok_out.append(p.dma("sync", d_yT, yT[:, :], deps=[tok_b2]))
            stb2.close()

        if "C2" in phases:
            stack_pbt.close()
            stack_r.close()
            pb67 = [psum(f"pb{i + 6}", [128, 512], F32) for i in range(2)]
            banks = pb + pb67
            stc2 = stack.enter_context(ExitStack())
            xnT2 = sb("xnT2", [128, NI * T], BF16, st=stc2)
            hTr = sb("hTr", [128, 4 * T], BF16, st=stc2)
            wcb = [sb(f"wcbuf{i}", [128, 512], BF16, st=stc2) for i in range(4)]
            xt = [sb(f"xt{i}", [128, 512], F32, st=stc2) for i in range(2)]
            ostage = [sb(f"ost{i}", [128, 512], F32, st=stc2) for i in range(2)]
            gsb3 = [sb(f"gsb3{i}", [128, 1024], BF16, st=stc2)
                    for i in range(2)]

            d_xnT2 = dramo("d_xnT2", [128, NI * T], BF16) \
                if "DBGC2" in phases else None
            d_bank0 = dramo("d_bank0", [128, 512], F32) \
                if "DBGC2" in phases else None
            dbg_m0tok = None
            d_bank = dramo("d_bank", [128, 512], F32) \
                if "DBGC2" in phases else None
            dep_c2 = [tok_b2] + p.qtoks("gpsimd") + p.qtoks("sync")
            if "C1b" in phases:
                prevgs = None
                gsfree = {0: None, 1: None}
                for m in range(32):
                    tg = p.dma("scalar", gsb3[m % 2][:, :],
                               gs_sp[m * 128:(m + 1) * 128, :],
                               deps=(dep_c2 if m < 2 else []) +
                               ([gsfree[m % 2]] if gsfree[m % 2] else []))
                    prevgs = p.op("vector", lambda e, m=m:
                                  e.scalar_tensor_tensor(
                                      yT[:, m * T:(m + 1) * T],
                                      yT[:, m * T:(m + 1) * T],
                                      gnorm[:, 0:1], gsb3[m % 2][:, :],
                                      OP.mult, OP.mult),
                                  deps=[tg, tok_b2], par=2, sig=True)
                    gsfree[m % 2] = prevgs
                tok_b2 = prevgs
                dep_c2 = [tok_b2] + dep_c2
            t_x2 = []
            for i in range(NI):
                t_x2.append(p.dma("sync", xnT2[:, i * T:(i + 1) * T],
                                  xn_tm[:, i * 128:(i + 1) * 128],
                                  deps=dep_c2 if i == 0 else [],
                                  transpose=True))
            t_h2 = []
            for r in range(4):
                t_h2.append(p.dma("sync", hTr[:, r * T:(r + 1) * T],
                                  hT_sp[r * 128:(r + 1) * 128, :],
                                  deps=dep_c2 if r == 0 else []))
            lhs_ready = [tok_b2] + t_x2[-8:] + t_h2

            def lhs_ap(m, tt):
                if m < 32:
                    return yT[:, m * T + tt * 128:m * T + (tt + 1) * 128]
                if m < 64:
                    i = m - 32
                    return xnT2[:, i * T + tt * 128:i * T + (tt + 1) * 128]
                r = m - 64
                return hTr[:, r * T + tt * 128:r * T + (tt + 1) * 128]

            bank_free3 = {i: None for i in range(8)}
            xt_last = {}
            ost_last = {}
            mlist = []
            if 0 in c2_parts:
                mlist += list(range(32))
            if 1 in c2_parts:
                mlist += list(range(32, 64))
            if 2 in c2_parts:
                mlist += list(range(64, 68))
            mmW = {}
            for ob in range(8):
                stop_toks = {}
                for mi, m in enumerate(mlist):
                    wb = wcb[mi % 4]
                    depw = [mmW.get((ob, mi - 4))] if mi >= 4 else (
                        [mmW.get((ob - 1, len(mlist) - 4 + mi))] if ob > 0
                        else list(dep_c2))
                    if m < 64:
                        src = (WC_bf[m * 128:(m + 1) * 128,
                                     ob * 512:(ob + 1) * 512] if m < 32 else
                               WC_bf[(m - 32) * 128:(m - 31) * 128,
                                     4096 + ob * 512:4096 + (ob + 1) * 512])
                        mrow = m if m < 32 else m - 32
                        tkw = p.dma("sync" if m % 2 == 0 else "scalar",
                                    wb[:, :], src,
                                    deps=depw + [wccast[mrow // 2]])
                    else:
                        r = m - 64
                        tkw = p.dma("sync", wb[:, :],
                                    upq[r * 128:(r + 1) * 128,
                                        ob * 512:(ob + 1) * 512],
                                    deps=depw + (tok_upq if m == 64 else []))
                    last = None
                    first = mi == 0
                    lastm = mi == len(mlist) - 1
                    for tt in range(8):
                        dmm = [tkw]
                        if first:
                            dmm += [bank_free3[tt]] + lhs_ready
                        last = p.op("tensor", lambda e, m=m, tt=tt, wb=wb,
                                    first=first, lastm=lastm:
                                    e.matmul(banks[tt][:, :], lhs_ap(m, tt),
                                             wb[:, :], start=first,
                                             stop=lastm),
                                    deps=dmm, sig=(tt == 7) or lastm)
                        if lastm:
                            stop_toks[tt] = last
                        if ob == 0 and mi == 0 and tt == 7:
                            dbg_m0tok = last
                    mmW[(ob, mi)] = last
                if ob == 0 and d_bank is not None:
                    tdbg = p.op("vector", lambda e: e.tensor_copy(
                        ostage[0][:, :], banks[0][:, :]),
                        deps=[stop_toks[0]], sig=True)
                    tok_out.append(p.dma("sync", d_bank, ostage[0][:, :],
                                         deps=[tdbg]))
                    tok_out.append(p.dma("sync", d_xnT2, xnT2[:, :],
                                         deps=[tdbg]))
                    xt_last[0] = tdbg
                    ost_last[0] = tok_out[-2]
                if ob == 0 and d_bank0 is not None:
                    tdb0 = p.op("vector", lambda e: e.tensor_copy(
                        ostage[1][:, :], banks[0][:, :]),
                        deps=[dbg_m0tok], sig=True)
                    tok_out.append(p.dma("sync", d_bank0, ostage[1][:, :],
                                         deps=[tdb0]))
                    xt_last[1] = tdb0
                    ost_last[1] = tok_out[-1]
                for tt in range(8):
                    bi = tt % 2
                    xt_war = xt_last.get(bi)
                    tx_ = p.dma("sync", xt[bi][:, :],
                                x_in[tt * 128:(tt + 1) * 128,
                                     ob * 512:(ob + 1) * 512],
                                deps=([bank_free3[tt]] if ob > 0 else
                                      list(dep_c2)) +
                                ([xt_war] if xt_war else []))
                    ost_war = ost_last.get(bi)
                    tadd = p.op("vector", lambda e, tt=tt, bi=bi:
                                e.tensor_add(ostage[bi][:, :],
                                             banks[tt][:, :], xt[bi][:, :]),
                                deps=[tx_, stop_toks[tt]] +
                                ([ost_war] if ost_war else []), sig=True)
                    tw = p.dma("scalar" if tt % 2 == 0 else "sync",
                               out_t[tt * 128:(tt + 1) * 128,
                                     ob * 512:(ob + 1) * 512],
                               ostage[bi][:, :], deps=[tadd])
                    tok_out.append(tw)
                    xt_last[bi] = tadd
                    ost_last[bi] = tw
                    bank_free3[tt] = tadd

        if "C2" not in phases:
            zb = sb("zb", [128, HID], F32)
            tz = p.op("vector", lambda e: e.memset(zb[:, :], 0.0), sig=True)
            for t in range(8):
                tok_out.append(p.dma(
                    "sync", out_t[t * 128:(t + 1) * 128, :], zb[:, :],
                    deps=[tz, tok_A1_last, tok_b2, tok_cc]))
        p.op("sync", lambda e: e.nop(),
             deps=tok_out[-24:] + p.qtoks("scalar") + p.qtoks("sync") +
             p.qtoks("gpsimd"))

        p.build(blk)
    return nc


def _evac_osub(p, sub, bank0, bank1, mmtoks, evf, evf2, rsc, evb, spill_free,
               cosb, sinb, t_cos, t_sin, kT, k_tm, v_tm, ident, pbt, ctx,
               q_sp, hT_sp, g_fm, dt_fm, dtbias, acol, t_aneg, evac_prev,
               pswapT=None, t_psw=None, pb45=None):
    tm0, tm1 = mmtoks
    qsub = sub < 32
    ksub = 32 <= sub < 40
    vsub = 40 <= sub < 48
    wsub = sub == 48
    dsub = 49 <= sub < 53

    if qsub or ksub:
        c0 = p.op("vector", lambda e: e.tensor_copy(evf[:, 0:512], bank0[:, :]),
                  deps=[tm0], sig=False)
        c1 = p.op("vector", lambda e: e.tensor_copy(evf[:, 512:1024], bank1[:, :]),
                  deps=[tm1], sig=True)
        bx = evb[0]
        rfree = ctx.get("rope_free")
        txs = p.op("vector", lambda e, bx=bx: e.tensor_mul(
            bx[:, :], evf[:, :], sinb[:, :]),
            deps=[t_sin, spill_free[0]], sig=True)
        spill_free[0] = txs
        mr0 = p.op("tensor", lambda e, bx=bx: e.matmul(
            pb45[0][:, :], pswapT[:, :], bx[:, 0:512], start=True, stop=True),
            deps=[txs, t_psw, rfree], sig=True)
        mr1 = p.op("tensor", lambda e, bx=bx: e.matmul(
            pb45[1][:, :], pswapT[:, :], bx[:, 512:1024],
            start=True, stop=True), sig=True)
        p.op("vector", lambda e: e.tensor_mul(evf2[:, :], evf[:, :], cosb[:, :]),
             deps=[t_cos])
        dst = evb[1]
        a0 = p.op("vector", lambda e, dst=dst: e.tensor_add(
            dst[:, 0:512], evf2[:, 0:512], pb45[0][:, :]),
            deps=[mr0, spill_free[1]], sig=True)
        rope_done = p.op("vector", lambda e, dst=dst: e.tensor_add(
            dst[:, 512:1024], evf2[:, 512:1024], pb45[1][:, :]),
            deps=[mr1], sig=True)
        ctx["rope_free"] = rope_done
        if qsub:
            tw = p.dma("gpsimd", q_sp[sub * 128:(sub + 1) * 128, :], dst[:, :],
                       deps=[rope_done])
            spill_free[1] = tw
            return c1
        kv = sub - 32
        ck = p.op("vector", lambda e, kv=kv, dst=dst: e.tensor_copy(
            kT[:, kv * T:(kv + 1) * T], dst[:, :]), deps=[rope_done], sig=True)
        spill_free[1] = ck
        last = None
        for n in range(8):
            pk = n % 2
            t1 = p.op("tensor", lambda e, kv=kv, n=n, pk=pk: e.transpose(
                pbt[pk][:, 0:128],
                kT[:, kv * T + n * 128:kv * T + (n + 1) * 128], ident[:, :]),
                deps=[ck, ctx["pbt_free"][pk]], sig=True)
            t2 = p.op("vector", lambda e, kv=kv, n=n, pk=pk: e.tensor_copy(
                k_tm[:, (kv * 8 + n) * 128:(kv * 8 + n + 1) * 128],
                pbt[pk][:, 0:128]), deps=[t1], sig=True)
            ctx["pbt_free"][pk] = t2
            last = t2
        ctx["ktm_done"] = last
        return c1

    if vsub:
        kv = sub - 40
        c0 = p.op("vector", lambda e: e.tensor_copy(evf[:, 0:512], bank0[:, :]),
                  deps=[tm0])
        c1 = p.op("vector", lambda e: e.tensor_copy(evf[:, 512:1024], bank1[:, :]),
                  deps=[tm1], sig=True)
        b = evb[0]
        cb = p.op("vector", lambda e, b=b: e.tensor_copy(b[:, :], evf[:, :]),
                  deps=[spill_free[0]], sig=True)
        spill_free[0] = cb
        last = None
        for n in range(8):
            pk = n % 2
            t1 = p.op("tensor", lambda e, n=n, pk=pk, b=b: e.transpose(
                pbt[pk][:, 128:256], b[:, n * 128:(n + 1) * 128], ident[:, :]),
                deps=[cb, ctx["pbt_free"][pk]], sig=True)
            t2 = p.op("vector", lambda e, kv=kv, n=n, pk=pk: e.tensor_copy(
                v_tm[:, (kv * 8 + n) * 128:(kv * 8 + n + 1) * 128],
                pbt[pk][:, 128:256]), deps=[t1], sig=True)
            ctx["pbt_free"][pk] = t2
            last = t2
        spill_free[0] = last
        ctx["vtm_done"] = last
        return c1

    if wsub:
        c0 = p.op("vector", lambda e: e.tensor_copy(evf[0:32, 0:512],
                                                    bank0[0:32, :]), deps=[tm0])
        c1 = p.op("vector", lambda e: e.tensor_copy(evf[0:32, 512:1024],
                                                    bank1[0:32, :]),
                  deps=[tm1], sig=True)
        p.op("vector", lambda e: e.tensor_scalar(
            evf[0:32, :], evf[0:32, :], dtbias[:, 0:1], None, OP.add))
        ta = p.op("vector", lambda e: e.tensor_scalar(
            evf2[0:32, :], evf[0:32, :], 0.0, None, OP.max), sig=True)
        a0 = p.op("scalar", lambda e: e.activation(
            rsc[0:32, :], evf[0:32, :], AF.Abs), deps=[ta], sig=True)
        p.op("scalar", lambda e: e.activation(
            rsc[0:32, :], rsc[0:32, :], AF.Exp, scale=-1.0))
        a2 = p.op("scalar", lambda e: e.activation(
            rsc[0:32, :], rsc[0:32, :], AF.Ln, bias=1.0), sig=True)
        a4 = p.op("vector", lambda e: e.tensor_add(
            dt_fm[:, :], rsc[0:32, :], evf2[0:32, :]), deps=[a2], sig=True)
        a5 = p.op("vector", lambda e: e.tensor_scalar(
            g_fm[:, :], dt_fm[:, :], acol[:, 0:1], None, OP.mult),
            deps=[t_aneg], sig=True)
        return a5

    if dsub:
        rr = sub - 49
        b = evb[rr % 2]
        c0 = p.op("vector", lambda e, b=b: e.tensor_copy(b[:, 0:512], bank0[:, :]),
                  deps=[tm0, spill_free[rr % 2]])
        c1 = p.op("vector", lambda e, b=b: e.tensor_copy(b[:, 512:1024],
                                                         bank1[:, :]),
                  deps=[tm1], sig=True)
        tw = p.dma("gpsimd", hT_sp[rr * 128:(rr + 1) * 128, :], b[:, :], deps=[c1])
        spill_free[rr % 2] = tw
        return c1
    raise AssertionError(sub)


def _rope_tables():
    inv_freq = 1.0 / ROPE_THETA ** (np.arange(0, D, 2, dtype=np.float64) / D)
    ang = np.arange(L, dtype=np.float64)[:, None] * inv_freq
    cos = np.concatenate([np.cos(ang), np.cos(ang)], axis=-1)
    sin = np.concatenate([np.sin(ang), np.sin(ang)], axis=-1)
    return cos.astype(np.float32), sin.astype(np.float32)


def _pswapT():
    P = np.zeros((128, 128), np.float32)
    for dst in range(128):
        if dst < 64:
            P[dst + 64, dst] = -1.0
        else:
            P[dst - 64, dst] = 1.0
    return P.astype(ml_dtypes.bfloat16)


def host_inputs(inputs):
    x = np.ascontiguousarray(np.asarray(inputs["L0_attn_hidden_states"],
                                        np.float32)[0])
    Wq = np.asarray(inputs["L0_attn_Wq"], np.float32)
    Wk = np.asarray(inputs["L0_attn_Wk"], np.float32)
    Wv = np.asarray(inputs["L0_attn_Wv"], np.float32)
    Wo = np.asarray(inputs["L0_attn_Wo"], np.float32)
    Wg = np.asarray(inputs["L0_attn_Wg"], np.float32)
    Win = np.asarray(inputs["L0_attn_Win"], np.float32)
    dt_bias = np.asarray(inputs["L0_attn_dt_bias"], np.float32)
    A_log = np.asarray(inputs["L0_attn_A_log_bias"], np.float32)
    gnorm = np.asarray(inputs["L0_attn_gnorm_w"], np.float32)
    W_org = np.asarray(inputs["L0_ffn_W_org"], np.float32)
    down = np.asarray(inputs["L0_ffn_lora_down"], np.float32)
    up = np.asarray(inputs["L0_ffn_lora_up"], np.float32)
    keys = np.asarray(inputs["L0_ffn_lora_keys"], np.float32)
    Wqf = np.asarray(inputs["L0_ffn_Wq"], np.float32)

    WA = np.ascontiguousarray(np.concatenate([
        Wq.T, Wk.T, Wv.T,
        np.concatenate([Win.T, np.zeros((HID, 96), np.float32)], axis=1),
        down.transpose(1, 0, 2).reshape(HID, 512),
        Wqf.T], axis=1))
    assert WA.shape == (HID, WA_COLS)
    cos, sin = _rope_tables()
    tri = np.ascontiguousarray(np.triu(np.ones((128, 128), np.float32)))
    Ihead = np.zeros((32, HID), np.float32)
    for h in range(32):
        Ihead[h, h * 128:(h + 1) * 128] = 1.0
    e127 = np.zeros((128, 1), np.float32)
    e127[127, 0] = 1.0
    Sel = np.zeros((8, 512), np.float32)
    for rr in range(4):
        for pp in range(128):
            Sel[2 * rr + pp // 64, rr * 128 + pp] = 1.0

    shared = {
        "WA": WA,
        "WG": np.ascontiguousarray(Wg.T),
        "WC": np.ascontiguousarray(np.concatenate([Wo.T, W_org.T], axis=1)),
        "UPT": np.ascontiguousarray(up.reshape(512, HID)),
        "keysT": np.ascontiguousarray(keys.T),
        "dt_bias": np.ascontiguousarray(dt_bias.reshape(32, 1)),
        "A_log": np.ascontiguousarray(A_log.reshape(32, 1)),
        "gnorm": np.ascontiguousarray(gnorm.reshape(128, 1)),
        "trimask": tri, "Ihead": Ihead, "e127": e127, "Sel": Sel,
        "ident_in": np.eye(128, dtype=ml_dtypes.bfloat16),
        "pswapT": _pswapT(),
        "identf_in": np.eye(128, dtype=np.float32),
    }
    in_maps = []
    for c in range(NCORE):
        m = dict(shared)
        m["x_c"] = np.ascontiguousarray(x[c * T:(c + 1) * T])
        m["costab"] = np.ascontiguousarray(cos[c * T:(c + 1) * T].T)
        m["sintab"] = np.ascontiguousarray(sin[c * T:(c + 1) * T].T)
        cm = np.zeros((1, 256), np.float32)
        for j in range(8):
            if j < c:
                cm[0, j * 32:(j + 1) * 32] = 1.0
        m["cmflat"] = cm
        m["omcmflat"] = np.ascontiguousarray(1.0 - cm)
        in_maps.append(m)
    return in_maps


def kernel(**inputs):
    in_maps = host_inputs(inputs)
    nc = build_nc()
    res = run_bass_kernel_spmd(nc, in_maps, core_ids=list(range(NCORE)))
    out = np.concatenate([np.asarray(res.results[c]["out"], np.float32)
                          for c in range(NCORE)], axis=0)
    return out.reshape(1, L, HID)


# version 06
# speedup vs baseline: 1.0723x; 1.0723x over previous
import numpy as np
import ml_dtypes
import concourse.bass as bass
import concourse.mybir as mybir
from concourse.bass_utils import run_bass_kernel_spmd
from contextlib import ExitStack

F32 = mybir.dt.float32
BF16 = mybir.dt.bfloat16
AF = mybir.ActivationFunctionType
OP = mybir.AluOpType
AXX = mybir.AxisListType.X

NCORE = 8
L = 8192
T = 1024
NCH = 8
H = 32
KVH = 8
D = 128
HID = 4096
NI = 32
ROPE_THETA = 10000.0

WA_COLS = 4096 + 1024 + 1024 + 128 + 512 + 128
ALL_PHASES = ("A0", "A1", "B1", "CC", "C1a", "B2", "C1b", "C2")
ENGS = ("tensor", "vector", "scalar", "gpsimd", "sync")


class Prog:

    def __init__(self, nc, stack):
        self.nc = nc
        self.stream = {e: [] for e in ENGS}
        self.msem = {e: stack.enter_context(nc.semaphore(f"ms_{e}")) for e in ENGS}
        self.mcount = {e: 0 for e in ENGS}
        self.waited = {e: {} for e in ENGS}
        self.dma_pool = {}
        for e in ("sync", "scalar", "gpsimd"):
            self.dma_pool[e] = [stack.enter_context(nc.semaphore(f"dq_{e}{i}"))
                                for i in range(8)]
        self.dma_n = {e: 0 for e in ("sync", "scalar", "gpsimd")}

    def _waits(self, eng, deps):
        out = []
        for d in deps:
            if d is None:
                continue
            sem, val = d
            key = sem.num if hasattr(sem, "num") else id(sem)
            if self.waited[eng].get(key, 0) < val:
                self.waited[eng][key] = val
                out.append((sem, val))
        return out

    SERIALIZE = {"vector": True, "scalar": True, "gpsimd": True,
                 "tensor": False, "sync": False}

    def op(self, eng, fn, deps=(), sig=False, par=1):
        ser = self.SERIALIZE[eng]
        deps = list(deps)
        if ser and self.mcount[eng] - (par - 1) > 0:
            deps.append((self.msem[eng], self.mcount[eng] - (par - 1)))
        waits = self._waits(eng, deps)
        tok = None
        if sig or ser:
            self.mcount[eng] += 1
            tok = (self.msem[eng], self.mcount[eng])
            msem = self.msem[eng]

            def clo(e, fn=fn, waits=waits, msem=msem):
                for s, v in waits:
                    e.wait_ge(s, v)
                fn(e).then_inc(msem, 1)
        else:
            def clo(e, fn=fn, waits=waits):
                for s, v in waits:
                    e.wait_ge(s, v)
                fn(e)
        self.stream[eng].append(clo)
        return tok

    def dma(self, eng, out, in_, deps=(), **kw):
        i = self.dma_n[eng]
        self.dma_n[eng] += 1
        sem = self.dma_pool[eng][i % 8]
        use = i // 8 + 1
        waits = self._waits(eng, deps)
        selfwait = (use - 1) * 16

        def clo(e, out=out, in_=in_, waits=waits, sem=sem, sw=selfwait, kw=kw):
            if sw > 0:
                e.wait_ge(sem, sw)
            for s, v in waits:
                e.wait_ge(s, v)
            e.dma_start(out=out, in_=in_, **kw).then_inc(sem, 16)
        self.stream[eng].append(clo)
        return (sem, use * 16)

    def qtoks(self, eng):
        n = self.dma_n[eng]
        toks = []
        for k in range(min(8, n)):
            i = n - 1 - k
            sem = self.dma_pool[eng][i % 8]
            toks.append((sem, (i // 8 + 1) * 16))
        return toks

    def build(self, block):
        for name in ENGS:
            items = self.stream[name]

            def make(items):
                def body(engine):
                    for clo in items:
                        clo(engine)
                return body
            getattr(block, name)(make(items))


def build_nc(phases=ALL_PHASES, dbg=(), c2_parts=(0, 1, 2)):
    nc = bass.Bass("TRN2", target_bir_lowering=False)
    phases = set(phases)
    dbg = set(dbg)

    def dram(name, shape, dtype):
        kind = "ExternalOutput" if name in dbg else "Internal"
        return nc.dram_tensor(name, shape, dtype, kind=kind).ap()

    x_in = nc.dram_tensor("x_c", [T, HID], F32, kind="ExternalInput").ap()
    WA = nc.dram_tensor("WA", [HID, WA_COLS], F32, kind="ExternalInput").ap()
    WG = nc.dram_tensor("WG", [HID, HID], F32, kind="ExternalInput").ap()
    WC = nc.dram_tensor("WC", [HID, 2 * HID], F32, kind="ExternalInput").ap()
    UPT_in = nc.dram_tensor("UPT", [512, HID], F32, kind="ExternalInput").ap()
    keysT_in = nc.dram_tensor("keysT", [128, 8], F32, kind="ExternalInput").ap()
    dtb_in = nc.dram_tensor("dt_bias", [32, 1], F32, kind="ExternalInput").ap()
    alog_in = nc.dram_tensor("A_log", [32, 1], F32, kind="ExternalInput").ap()
    gnorm_in = nc.dram_tensor("gnorm", [128, 1], F32, kind="ExternalInput").ap()
    cost_in = nc.dram_tensor("costab", [128, T], F32, kind="ExternalInput").ap()
    sint_in = nc.dram_tensor("sintab", [128, T], F32, kind="ExternalInput").ap()
    tri_in = nc.dram_tensor("trimask", [128, 128], F32, kind="ExternalInput").ap()
    ihead_in = nc.dram_tensor("Ihead", [32, HID], F32, kind="ExternalInput").ap()
    e127_in = nc.dram_tensor("e127", [128, 1], F32, kind="ExternalInput").ap()
    sel_in = nc.dram_tensor("Sel", [8, 512], F32, kind="ExternalInput").ap()
    cmf_in = nc.dram_tensor("cmflat", [1, 256], F32, kind="ExternalInput").ap()
    omcmf_in = nc.dram_tensor("omcmflat", [1, 256], F32, kind="ExternalInput").ap()
    id_in = nc.dram_tensor("ident_in", [128, 128], BF16, kind="ExternalInput").ap()
    idf_in = nc.dram_tensor("identf_in", [128, 128], F32, kind="ExternalInput").ap()
    pswap_in = nc.dram_tensor("pswapT", [128, 128], BF16, kind="ExternalInput").ap()

    out_t = nc.dram_tensor("out", [T, HID], F32, kind="ExternalOutput").ap()

    xn_tm = dram("xn_tm", [T, HID], BF16)
    q_sp = dram("q_sp", [HID, T], BF16)
    hT_sp = dram("hT_sp", [512, T], BF16)
    gs_sp = dram("gs_sp", [HID, T], BF16)
    g_bounce = dram("g_bounce", [NCH, HID], F32)
    r_bounce = dram("r_bounce", [4, 512], F32)
    pay = dram("pay", [128, 4160], F32)
    gath = dram("gath", [128 * NCORE, 4160], F32)
    upq = dram("upq", [512, HID], BF16)
    WG_bf = dram("WG_bf", [HID, HID], BF16)
    WC_bf = dram("WC_bf", [HID, 2 * HID], BF16)
    yh_dbg = dram("yh_dbg", [HID, T], BF16) if "yh_dbg" in dbg else None
    PAY_XS, PAY_P = 4096, 4128

    stack = ExitStack()
    with stack:
        def sb(name, shape, dtype, side="left", st=None):
            return (st or stack).enter_context(
                nc.sbuf_tensor(name, shape, dtype, side=side))

        def psum(name, shape, dtype):
            return stack.enter_context(nc.psum_tensor(name, shape, dtype))

        stack_r = stack.enter_context(ExitStack())

        def sb_r(name, shape, dtype):
            return sb(name, shape, dtype, side="right", st=stack_r)
        tri = sb("tri", [128, 128], F32)
        pswapT = sb("pswapT_s", [128, 128], BF16)
        tri_bf = sb("tri_bf", [128, 128], BF16)
        ident = sb("ident", [128, 128], BF16)
        identf = sb("identf", [128, 128], F32)
        gnorm = sb("gnormc", [128, 1], F32)
        e127 = sb("e127s", [128, 1], F32)
        ones_r = sb("ones_r", [1, 128], F32)
        consts = sb("consts", [128, 8], F32)
        C_I4096, C_EPS, C_I128, C_QS, C_ZERO, C_ONE = range(6)

        kT = sb_r("kT", [128, KVH * T], BF16)
        k_tm = sb_r("k_tm", [128, KVH * NCH * D], BF16)
        v_tm = sb_r("v_tm", [128, KVH * NCH * D], BF16)
        dt_tm = sb_r("dt_tm", [128, NCH * 32], F32)
        G_tm = sb_r("G_tm", [128, NCH * 32], F32)
        expG_tm = sb_r("expG_tm", [128, NCH * 32], F32)
        edec_tm = sb_r("edec_tm", [128, NCH * 32], F32)
        dec_bc = sb_r("dec_bc", [128, NCH * 32], F32)
        fut_bc = sb_r("fut_bc", [128, NCH * 32], F32)
        rows = sb_r("rows", [1, 576], F32)
        state = sb_r("state", [128, H * D], F32)
        xsum_c = sb_r("xsum_c", [128, 32], F32)
        acol = sb_r("acol", [32, 1], F32)
        dtbias = sb_r("dtbias", [32, 1], F32)
        stack_r2 = stack.enter_context(ExitStack())
        g_fm = sb("g_fm", [32, T], F32, side="right", st=stack_r2)
        dt_fm = sb("dt_fm", [32, T], F32, side="right", st=stack_r2)

        pb = [psum(f"pb{i}", [128, 512], F32) for i in range(6)]
        stack_pbt = stack.enter_context(ExitStack())
        pbt = [stack_pbt.enter_context(nc.psum_tensor("pbt0", [128, 1024], BF16)),
               stack_pbt.enter_context(nc.psum_tensor("pbt1", [128, 1024], BF16))]

        cc_sem = stack.enter_context(nc.semaphore("cc_sem"))
        blk = stack.enter_context(nc.Block())
        p = Prog(nc, stack)
        ctx = {"pbt_free": {0: None, 1: None}}

        t_tri = p.dma("sync", tri[:, :], tri_in)
        p.dma("sync", gnorm[:, :], gnorm_in)
        t_gn = p.dma("sync", e127[:, :], e127_in)
        t_dtb = p.dma("sync", dtbias[:, :], dtb_in)
        t_id = p.dma("sync", ident[:, :], id_in)
        t_idf = p.dma("sync", identf[:, :], idf_in)
        t_psw = p.dma("sync", pswapT[:, :], pswap_in)
        t_alog = p.dma("sync", acol[:, :], alog_in)
        t_tribf = p.op("vector", lambda e: e.tensor_copy(tri_bf[:, :], tri[:, :]),
                       deps=[t_tri], sig=True)
        p.op("vector", lambda e: e.memset(ones_r[:, :], 1.0))
        p.op("vector", lambda e: e.memset(consts[:, C_I4096:C_I4096 + 1], 1.0 / 4096))
        p.op("vector", lambda e: e.memset(consts[:, C_EPS:C_EPS + 1], 1e-5))
        p.op("vector", lambda e: e.memset(consts[:, C_I128:C_I128 + 1], 1.0 / 128))
        p.op("vector", lambda e: e.memset(
            consts[:, C_QS:C_QS + 1], 1.0 / (L * float(np.sqrt(128.0)))))
        p.op("vector", lambda e: e.memset(consts[:, C_ZERO:C_ZERO + 1], 0.0))
        t_consts = p.op("vector", lambda e: e.memset(
            consts[:, C_ONE:C_ONE + 1], 1.0), sig=True)
        t_aexp = p.op("scalar", lambda e: e.activation(
            acol[:, :], acol[:, :], AF.Exp), deps=[t_alog], sig=True)
        t_aneg = p.op("vector", lambda e: e.tensor_scalar(
            acol[:, :], acol[:, :], -1.0, None, OP.mult), deps=[t_aexp], sig=True)

        tok_xnT = []
        stack_xnT = stack.enter_context(ExitStack())
        xnT = sb("xnT", [128, NI * T], BF16, st=stack_xnT)
        stack_a = stack.enter_context(ExitStack())
        if "A0" in phases:
            xbuf = [sb(f"xbuf{i}", [128, HID], F32, st=stack_a)
                    for i in range(1)]
            xsq = sb("xsq", [128, HID // 4], F32, st=stack_a)
            xnb = sb("xnb", [128, HID], BF16, st=stack_a)
            lncol = sb("lncol", [128, 8], F32, st=stack_a)
            xn_w = []
            for t in range(8):
                b = xbuf[0]
                tl = p.dma("sync", b[:, :], x_in[t * 128:(t + 1) * 128, :],
                           deps=[xn_w[t - 1]] if t >= 1 else [])
                ts_ = p.op("vector", lambda e, b=b: e.tensor_reduce(
                    lncol[:, 0:1], b[:, :], AXX, OP.add), deps=[tl], sig=True)
                tprev = tl
                for qq in range(4):
                    tq = p.op("scalar", lambda e, b=b, qq=qq: e.activation(
                        xsq[:, :], b[:, qq * 1024:(qq + 1) * 1024], AF.Square),
                        deps=[tprev], sig=True)
                    tprev = p.op("vector", lambda e, qq=qq: e.tensor_reduce(
                        lncol[:, 4 + qq:5 + qq], xsq[:, :], AXX, OP.add),
                        deps=[tq], sig=True)
                p.op("vector", lambda e: e.tensor_add(
                    lncol[:, 4:5], lncol[:, 4:5], lncol[:, 5:6]))
                p.op("vector", lambda e: e.tensor_add(
                    lncol[:, 6:7], lncol[:, 6:7], lncol[:, 7:8]))
                tss = p.op("vector", lambda e: e.tensor_add(
                    lncol[:, 1:2], lncol[:, 4:5], lncol[:, 6:7]), sig=True)
                tm = p.op("vector", lambda e: e.tensor_scalar(
                    lncol[:, 2:3], lncol[:, 0:1], consts[:, C_I4096:C_I4096 + 1],
                    None, OP.mult), deps=[t_consts], sig=False)
                tv = p.op("vector", lambda e: e.tensor_scalar(
                    lncol[:, 1:2], lncol[:, 1:2], consts[:, C_I4096:C_I4096 + 1],
                    None, OP.mult))
                tv2 = p.op("vector", lambda e: e.scalar_tensor_tensor(
                    lncol[:, 3:4], lncol[:, 2:3], lncol[:, 2:3], lncol[:, 1:2],
                    OP.mult, OP.subtract), sig=True)
                tstd = p.op("scalar", lambda e: e.activation(
                    lncol[:, 3:4], lncol[:, 3:4], AF.Sqrt, scale=-1.0,
                    bias=consts[:, C_EPS:C_EPS + 1]), deps=[tv2], sig=True)
                trs = p.op("vector", lambda e: e.reciprocal(
                    lncol[:, 3:4], lncol[:, 3:4]), deps=[tstd], sig=True)
                txn = p.op("vector", lambda e, b=b: e.tensor_scalar(
                    xnb[:, :], b[:, :], lncol[:, 2:3], lncol[:, 3:4],
                    OP.subtract, OP.mult),
                    deps=[xn_w[t - 1]] if t >= 1 else [], sig=True)
                tw = p.dma("gpsimd", xn_tm[t * 128:(t + 1) * 128, :], xnb[:, :],
                           deps=[txn])
                xn_w.append(tw)
            for i in range(NI):
                tt = p.dma("sync", xnT[:, i * T:(i + 1) * T],
                           xn_tm[:, i * 128:(i + 1) * 128],
                           deps=xn_w if i == 0 else [], transpose=True)
                tok_xnT.append(tt)

        tok_A1_last = None
        tok_gdt = None
        tok_kv_done = []
        tok_wqf = []
        tok_xsum = None
        if "A1" in phases:
            wbuf = [sb(f"wbuf{i}", [128, 256], BF16, st=stack_a)
                    for i in range(4)]
            evf = sb("evf", [128, 1024], F32, st=stack_a)
            evf2 = sb("evf2", [128, 1024], F32, st=stack_a)
            rsc = sb("rsc", [128, 1024], F32, st=stack_a)
            evb = [sb(f"evb{i}", [128, 1024], BF16, st=stack_a)
                   for i in range(2)]
            cosb = sb("cosb", [128, T], BF16, st=stack_a)
            sinb = sb("sinb", [128, T], BF16, st=stack_a)
            t_cos = p.dma("gpsimd", cosb[:, :], cost_in)
            t_sin = p.dma("gpsimd", sinb[:, :], sint_in)

            xs_last = None
            for i in range(NI):
                xs_last = p.op("vector", lambda e, i=i: e.tensor_reduce(
                    xsum_c[:, i:i + 1], xnT[:, i * T:(i + 1) * T], AXX, OP.add),
                    deps=[tok_xnT[i]] if tok_xnT else [], sig=(i == NI - 1))
            tok_xsum = xs_last

            NOG = WA_COLS // 256
            mmtok = {}
            wcopy = {}
            bank_free = {s: None for s in range(6)}
            wgcast = []
            wccast = []
            spill_free = [None, None]
            evac_prev = None
            for og in range(NOG):
                c0 = og * 256
                subs = [og * 2 + s for s in range(2)]
                mm_last = [None, None]
                for i in range(NI):
                    wb = wbuf[i % 4]
                    if i >= 4:
                        pk = (og, i - 4)
                    elif og > 0:
                        pk = (og - 1, NI - 4 + i)
                    else:
                        pk = None
                    depw = [mmtok.get(pk), wcopy.get(pk)] if pk else []
                    tkw = p.dma("gpsimd", wb[:, :],
                                WA[i * 128:(i + 1) * 128, c0:c0 + 256], deps=depw)
                    last = None
                    nsubs = [s for s in range(2) if subs[s] != 53]
                    for s in nsubs:
                        deps_mm = [tkw]
                        if i == 0:
                            deps_mm += [bank_free[2 * s], bank_free[2 * s + 1]]
                            if tok_xnT:
                                deps_mm += tok_xnT[-8:]
                        islast = s == nsubs[-1]
                        tm0 = p.op("tensor", lambda e, wb=wb, s=s, i=i: e.matmul(
                            pb[2 * s][:, :], wb[:, s * 128:(s + 1) * 128],
                            xnT[:, i * T:i * T + 512],
                            start=(i == 0), stop=(i == NI - 1)),
                            deps=deps_mm, sig=(i == NI - 1))
                        tm1 = p.op("tensor", lambda e, wb=wb, s=s, i=i: e.matmul(
                            pb[2 * s + 1][:, :], wb[:, s * 128:(s + 1) * 128],
                            xnT[:, i * T + 512:(i + 1) * T],
                            start=(i == 0), stop=(i == NI - 1)),
                            sig=(i == NI - 1) or islast)
                        if i == NI - 1:
                            mm_last[s] = (tm0, tm1)
                        if islast:
                            last = tm1
                    mmtok[(og, i)] = last
                for s in range(2):
                    sub = subs[s]
                    if sub == 53:
                        continue
                    ev = _evac_osub(
                        p, sub, pb[2 * s], pb[2 * s + 1], mm_last[s],
                        evf, evf2, rsc, evb, spill_free, cosb, sinb,
                        t_cos, t_sin, kT, k_tm, v_tm, ident, pbt, ctx,
                        q_sp, hT_sp, g_fm, dt_fm, dtbias, acol, t_aneg,
                        evac_prev, pswapT=pswapT, t_psw=t_psw,
                        pb45=(pb[4], pb[5]))
                    bank_free[2 * s] = ev
                    bank_free[2 * s + 1] = ev
                    evac_prev = ev
            tok_A1_last = evac_prev

        stack_a.close()

        tok_out = []
        dramo = lambda n, s, d: nc.dram_tensor(n, s, d,
                                               kind="ExternalOutput").ap()
        if "DBG1" in phases:
            d_kT = dramo("d_kT", [128, KVH * T], BF16)
            d_ktm = dramo("d_ktm", [128, KVH * NCH * D], BF16)
            d_vtm = dramo("d_vtm", [128, KVH * NCH * D], BF16)
            d_dt = dramo("d_dt", [32, T], F32)
            d_g = dramo("d_g", [32, T], F32)
            d_xsum = dramo("d_xsum", [128, 32], F32)
            dl = [tok_A1_last, tok_xsum]
            tok_out.append(p.dma("sync", d_kT, kT[:, :], deps=dl))
            tok_out.append(p.dma("sync", d_ktm, k_tm[:, :], deps=dl))
            tok_out.append(p.dma("sync", d_vtm, v_tm[:, :], deps=dl))
            tok_out.append(p.dma("sync", d_dt, dt_fm[:, :], deps=dl))
            tok_out.append(p.dma("sync", d_g, g_fm[:, :], deps=dl))
            tok_out.append(p.dma("sync", d_xsum, xsum_c[:, :], deps=dl))

        tok_pay = []
        tok_b1 = None
        if "B1" in phases:
            stb = stack.enter_context(ExitStack())
            g_sc = sb("g_sc", [128, 32], F32, st=stb)
            gt_sc = sb("gt_sc", [32, 128], F32, st=stb)
            xcb = [sb(f"xcb{i}", [128, 128], BF16, st=stb) for i in range(2)]
            kdb = [sb(f"kdb{i}", [128, 128], BF16, st=stb) for i in range(2)]
            dep0 = [tok_A1_last, t_idf, t_tri]
            prev4 = None
            prev5 = None
            tok_gb = []
            for n in range(NCH):
                t1 = p.op("tensor", lambda e, n=n: e.transpose(
                    pb[4][:, 0:32], g_fm[:, n * 128:(n + 1) * 128],
                    identf[0:32, 0:32]), deps=dep0 + [prev4], sig=True)
                t2 = p.op("vector", lambda e: e.tensor_copy(
                    g_sc[:, :], pb[4][:, 0:32]), deps=[t1], sig=True)
                t3 = p.op("tensor", lambda e, n=n: e.transpose(
                    pb[4][:, 32:64], dt_fm[:, n * 128:(n + 1) * 128],
                    identf[0:32, 0:32]), deps=[t2], sig=True)
                prev4 = p.op("vector", lambda e, n=n: e.tensor_copy(
                    dt_tm[:, n * 32:(n + 1) * 32], pb[4][:, 32:64]),
                    deps=[t3], sig=True)
                m1 = p.op("tensor", lambda e: e.matmul(
                    pb[5][:, 0:32], tri[:, :], g_sc[:, :],
                    start=True, stop=True), deps=[t2, prev5], sig=True)
                gv = p.op("vector", lambda e, n=n: e.tensor_copy(
                    G_tm[:, n * 32:(n + 1) * 32], pb[5][:, 0:32]),
                    deps=[m1], sig=True)
                m2 = p.op("tensor", lambda e: e.matmul(
                    pb[5][0:32, 64:192], g_sc[:, :], tri[:, :],
                    start=True, stop=True), deps=[gv], sig=True)
                gtv = p.op("vector", lambda e: e.tensor_copy(
                    gt_sc[:, :], pb[5][0:32, 64:192]),
                    deps=[m2] + tok_gb[-1:], sig=True)
                p.op("scalar", lambda e, n=n: e.activation(
                    expG_tm[:, n * 32:(n + 1) * 32],
                    G_tm[:, n * 32:(n + 1) * 32], AF.Exp), deps=[gv], sig=True)
                m3 = p.op("tensor", lambda e: e.matmul(
                    pb[5][0:1, 192:224], gt_sc[:, 127:128], identf[0:32, 0:32],
                    start=True, stop=True), deps=[gtv], sig=True)
                cs_r = p.op("vector", lambda e: e.tensor_copy(
                    rows[0:1, 544:576], pb[5][0:1, 192:224]),
                    deps=[m3], sig=True)
                p.op("scalar", lambda e, n=n: e.activation(
                    rows[0:1, n * 32:(n + 1) * 32], rows[0:1, 544:576],
                    AF.Exp), deps=[cs_r], sig=True)
                m4 = p.op("tensor", lambda e: e.matmul(
                    pb[5][:, 224:256], ones_r[0:1, 0:128], rows[0:1, 544:576],
                    start=True, stop=True), deps=[cs_r], sig=True)
                sb1 = p.op("vector", lambda e, n=n: e.tensor_sub(
                    edec_tm[:, n * 32:(n + 1) * 32], pb[5][:, 224:256],
                    G_tm[:, n * 32:(n + 1) * 32]), deps=[m4], sig=True)
                p.op("scalar", lambda e, n=n: e.activation(
                    edec_tm[:, n * 32:(n + 1) * 32],
                    edec_tm[:, n * 32:(n + 1) * 32], AF.Exp),
                    deps=[sb1], sig=True)
                prev5 = sb1
                tok_gb.append(p.dma(
                    "sync",
                    g_bounce[n:n + 1, :].rearrange("a (h i) -> (a h) i", h=32),
                    gt_sc[:, :], deps=[gtv]))
            p.op("vector", lambda e: e.memset(rows[0:1, 256 + 224:256 + 256],
                                              1.0))
            for n in range(6, -1, -1):
                p.op("vector", lambda e, n=n: e.tensor_mul(
                    rows[0:1, 256 + n * 32:256 + (n + 1) * 32],
                    rows[0:1, 256 + (n + 1) * 32:256 + (n + 2) * 32],
                    rows[0:1, (n + 1) * 32:(n + 2) * 32]))
            tP = p.op("vector", lambda e: e.tensor_mul(
                rows[0:1, 512:544], rows[0:1, 256:288], rows[0:1, 0:32]),
                sig=True)
            prevb = tP
            for n in range(NCH):
                mm_d = p.op("tensor", lambda e, n=n: e.matmul(
                    pb[5][:, 256:288], ones_r[0:1, 0:128],
                    rows[0:1, n * 32:(n + 1) * 32], start=True, stop=True),
                    deps=[prevb], sig=True)
                ev_d = p.op("vector", lambda e, n=n: e.tensor_copy(
                    dec_bc[:, n * 32:(n + 1) * 32], pb[5][:, 256:288]),
                    deps=[mm_d], sig=True)
                mm_f = p.op("tensor", lambda e, n=n: e.matmul(
                    pb[5][:, 288:320], ones_r[0:1, 0:128],
                    rows[0:1, 256 + n * 32:256 + (n + 1) * 32],
                    start=True, stop=True), deps=[ev_d], sig=True)
                prevb = p.op("vector", lambda e, n=n: e.tensor_copy(
                    fut_bc[:, n * 32:(n + 1) * 32], pb[5][:, 288:320]),
                    deps=[mm_f], sig=True)
            sprev = {0: None, 1: None}
            mmS = {}
            for h in range(H):
                kv = h // 4
                pk = h % 2
                mlast = None
                for n in range(NCH):
                    bi = n % 2
                    war = [mmS.get((h, n - 2)), mmS.get((h - 1, NCH - 2 + n))]
                    tx = p.op("vector", lambda e, kv=kv, n=n, h=h, bi=bi:
                              e.tensor_scalar(
                                  xcb[bi][:, :],
                                  v_tm[:, (kv * 8 + n) * 128:(kv * 8 + n + 1) * 128],
                                  dt_tm[:, n * 32 + h:n * 32 + h + 1],
                                  None, OP.mult),
                              deps=[d for d in war if d] + [prevb], sig=True)
                    tk = p.op("vector", lambda e, kv=kv, n=n, h=h, bi=bi:
                              e.tensor_scalar(
                                  kdb[bi][:, :],
                                  k_tm[:, (kv * 8 + n) * 128:(kv * 8 + n + 1) * 128],
                                  edec_tm[:, n * 32 + h:n * 32 + h + 1],
                                  fut_bc[:, n * 32 + h:n * 32 + h + 1],
                                  OP.mult, OP.mult), sig=True)
                    mlast = p.op("tensor", lambda e, h=h, n=n, bi=bi, pk=pk:
                                 e.matmul(pb[pk][:, 0:128], kdb[bi][:, :],
                                          xcb[bi][:, :], start=(n == 0),
                                          stop=(n == NCH - 1)),
                                 deps=[tx, tk] +
                                 ([sprev[pk]] if n == 0 else []), sig=True)
                    mmS[(h, n)] = mlast
                sprev[pk] = p.op("vector", lambda e, h=h, pk=pk: e.tensor_copy(
                    state[:, h * 128:(h + 1) * 128], pb[pk][:, 0:128]),
                    deps=[mlast], sig=True)
            tok_b1 = sprev[1]
            tok_pay.append(p.dma("sync", pay[:, 0:4096], state[:, :],
                                 deps=[sprev[0], sprev[1]]))
            tok_pay.append(p.dma("sync", pay[:, PAY_XS:PAY_XS + 32],
                                 xsum_c[:, :], deps=[tok_xsum]))
            tok_pay.append(p.dma("sync", pay[0:1, PAY_P:PAY_P + 32],
                                 rows[0:1, 512:544], deps=[tP]))
            stb.close()
            stack_r2.close()

        if "DBGB1" in phases:
            d_state = dramo("d_state", [128, H * D], F32)
            d_rows = dramo("d_rows", [1, 576], F32)
            d_Gtm = dramo("d_Gtm", [128, NCH * 32], F32)
            d_edec = dramo("d_edec", [128, NCH * 32], F32)
            d_decbc = dramo("d_decbc", [128, NCH * 32], F32)
            tok_out.append(p.dma("sync", d_state, state[:, :], deps=[tok_b1]))
            tok_out.append(p.dma("sync", d_rows, rows[:, :], deps=[tok_b1]))
            tok_out.append(p.dma("sync", d_Gtm, G_tm[:, :], deps=[tok_b1]))
            tok_out.append(p.dma("sync", d_edec, edec_tm[:, :], deps=[tok_b1]))
            tok_out.append(p.dma("sync", d_decbc, dec_bc[:, :], deps=[tok_b1]))

        tok_cc = None
        tcc = None
        if "CC" in phases:
            tcc = p.op("gpsimd", lambda e: e.collective_compute(
                "AllGather", OP.bypass,
                replica_groups=[list(range(NCORE))],
                ins=[pay.opt()], outs=[gath.opt()]), deps=tok_pay, sig=True)

        tok_c1a = None
        if "C1a" in phases:
            stc = stack.enter_context(ExitStack())
            wbuf2 = [sb(f"wbufc{i}", [128, 256], BF16, st=stc) for i in range(4)]
            gsb = [sb(f"gsb{i}", [128, 1024], BF16, st=stc) for i in range(2)]
            ogs = [(og * 2, og * 256, 2) for og in range(16)]
            mmtok2 = {}
            bank_free2 = {s: tok_b1 for s in range(6)}
            gs_free = [None, None]
            guard0 = [tok_A1_last, tok_b1] + p.qtoks("gpsimd")
            ev2 = None
            for ogi, (s0, c0, ns) in enumerate(ogs):
                mm_last2 = {}
                for i in range(NI):
                    wb = wbuf2[i % 4]
                    if i >= 4:
                        pkk = (ogi, i - 4)
                    elif ogi > 0:
                        pkk = (ogi - 1, NI - 4 + i)
                    else:
                        pkk = None
                    depw = [mmtok2.get(pkk)] if pkk else list(guard0)
                    tkw = p.dma("gpsimd", wb[:, 0:128 * ns],
                                WG[i * 128:(i + 1) * 128, c0:c0 + 128 * ns],
                                deps=depw)
                    last = None
                    for s in range(ns):
                        deps_mm = [tkw]
                        if i == 0:
                            deps_mm += [bank_free2[2 * s], bank_free2[2 * s + 1]]
                        tm0 = p.op("tensor", lambda e, wb=wb, s=s, i=i: e.matmul(
                            pb[2 * s][:, :], wb[:, s * 128:(s + 1) * 128],
                            xnT[:, i * T:i * T + 512],
                            start=(i == 0), stop=(i == NI - 1)),
                            deps=deps_mm, sig=(i == NI - 1))
                        tm1 = p.op("tensor", lambda e, wb=wb, s=s, i=i: e.matmul(
                            pb[2 * s + 1][:, :], wb[:, s * 128:(s + 1) * 128],
                            xnT[:, i * T + 512:(i + 1) * T],
                            start=(i == 0), stop=(i == NI - 1)),
                            sig=(i == NI - 1) or (s == ns - 1))
                        if i == NI - 1:
                            mm_last2[s] = (tm0, tm1)
                        if s == ns - 1:
                            last = tm1
                    mmtok2[(ogi, i)] = last
                for s in range(ns):
                    sub = s0 + s
                    tm0, tm1 = mm_last2[s]
                    gb = gsb[sub % 2]
                    a0 = p.op("scalar", lambda e, gb=gb, s=s: e.activation(
                        gb[:, 0:512], pb[2 * s][:, :], AF.Silu),
                        deps=[tm0, gs_free[sub % 2]], sig=True)
                    a1 = p.op("scalar", lambda e, gb=gb, s=s: e.activation(
                        gb[:, 512:1024], pb[2 * s + 1][:, :], AF.Silu),
                        deps=[tm1], sig=True)
                    tw = p.dma("gpsimd", gs_sp[sub * 128:(sub + 1) * 128, :],
                               gb[:, :], deps=[a1])
                    gs_free[sub % 2] = tw
                    bank_free2[2 * s] = a1
                    bank_free2[2 * s + 1] = a1
                    ev2 = a1
            tok_c1a = ev2
            c1a_done = [tok_c1a, mmtok2[(len(ogs) - 1, NI - 1)]] + \
                p.qtoks("gpsimd")
            stc.close()
        else:
            c1a_done = [tok_b1]
        if tcc is not None:
            tok_cc = p.op("gpsimd", lambda e: e.nop(), deps=[tcc], sig=True)
        stack_xnT.close()

        tok_pre = None
        tok_upq = []
        if "B2" in phases:
            stp = stack.enter_context(ExitStack())
            gstage = [sb(f"gstage{i}", [128, HID // 2], F32, st=stp) for i in range(2)]
            paux = sb("paux", [1, 1024], F32, st=stp)
            cmf_s = sb("cmf_s", [1, 256], F32, st=stp)
            omcmf_s = sb("omcmf_s", [1, 256], F32, st=stp)
            wbc = sb("wbc", [128, 256], F32, st=stp)
            keys_s = sb("keys_s", [128, 8], F32, st=stp)
            sel_s = sb("sel_s", [8, 512], F32, st=stp)
            wqfb = [sb(f"wqfb{i}", [128, 128], BF16, st=stp) for i in range(2)]
            xst = sb("xst", [128, 32], F32, st=stp)
            xsum_bf = sb("xsum_bf", [128, 32], BF16, st=stp)
            ce_cols = sb("ce_cols", [128, 4], F32, st=stp)
            upb = [sb(f"upb{i}", [128, 2048], BF16, st=stp) for i in range(2)]
            qrow = sb("qrow", [1, 128], F32, st=stp)
            qcol = sb("qcol", [128, 1], F32, st=stp)
            ccol = sb("ccol", [8, 1], F32, st=stp)

            p.dma("sync", cmf_s[:, :], cmf_in, deps=[tok_cc] + c1a_done)
            p.dma("sync", omcmf_s[:, :], omcmf_in)
            p.dma("sync", keys_s[:, :], keysT_in)
            t_sel = p.dma("sync", sel_s[:, :], sel_in)
            tpr = []
            for j in range(NCORE):
                tpr.append(p.dma(
                    "sync", paux[0:1, j * 32:(j + 1) * 32],
                    gath[j * 128:j * 128 + 1, PAY_P:PAY_P + 32],
                    deps=[tok_cc] + c1a_done))
            p.op("vector", lambda e: e.tensor_mul(
                paux[0:1, 256:512], paux[0:1, 0:256], cmf_s[0:1, :]),
                deps=tpr + [tok_cc])
            p.op("vector", lambda e: e.tensor_add(
                paux[0:1, 256:512], paux[0:1, 256:512], omcmf_s[0:1, :]))
            p.op("vector", lambda e: e.memset(paux[0:1, 768:800], 1.0))
            for j in range(7, -1, -1):
                p.op("vector", lambda e, j=j: e.tensor_mul(
                    paux[0:1, 512 + j * 32:512 + (j + 1) * 32],
                    cmf_s[0:1, j * 32:(j + 1) * 32], paux[0:1, 768:800]))
                p.op("vector", lambda e, j=j: e.tensor_mul(
                    paux[0:1, 768:800], paux[0:1, 768:800],
                    paux[0:1, 256 + j * 32:256 + (j + 1) * 32]))
            tw_rows = p.op("vector", lambda e: e.nop(), sig=True)
            prevw = tw_rows
            for j in range(NCORE):
                mmw = p.op("tensor", lambda e, j=j: e.matmul(
                    pb[5][:, 320:352], ones_r[0:1, 0:128],
                    paux[0:1, 512 + j * 32:512 + (j + 1) * 32],
                    start=True, stop=True), deps=[prevw], sig=True)
                prevw = p.op("vector", lambda e, j=j: e.tensor_copy(
                    wbc[:, j * 32:(j + 1) * 32], pb[5][:, 320:352]),
                    deps=[mmw], sig=True)
            tz = p.op("vector", lambda e: e.memset(state[:, :], 0.0),
                      deps=[tok_pay[0] if tok_pay else None, tok_cc] +
                      c1a_done, sig=True)
            prevg = tz
            ghalf_done = {0: None, 1: None}
            for j in range(NCORE):
                for gh in range(2):
                    gs = gstage[gh]
                    tg = p.dma("sync", gs[:, :],
                               gath[j * 128:(j + 1) * 128,
                                    gh * 2048:(gh + 1) * 2048],
                               deps=[ghalf_done[gh], tok_cc, tz])
                    for hq in range(H // 2):
                        h = gh * 16 + hq
                        prevg = p.op("vector", lambda e, j=j, h=h, hq=hq,
                                     gs=gs: e.scalar_tensor_tensor(
                                         state[:, h * 128:(h + 1) * 128],
                                         gs[:, hq * 128:(hq + 1) * 128],
                                         wbc[:, j * 32 + h:j * 32 + h + 1],
                                         state[:, h * 128:(h + 1) * 128],
                                         OP.mult, OP.add),
                                     deps=[tg, prevw], par=8,
                                     sig=(hq == H // 2 - 1))
                    ghalf_done[gh] = prevg
            tok_in = prevg
            prevx = None
            for j in range(NCORE):
                tg = p.dma("sync", xst[:, :],
                           gath[j * 128:(j + 1) * 128, PAY_XS:PAY_XS + 32],
                           deps=[tok_cc, prevx, tok_pay[1] if len(tok_pay) > 1
                                 else None])
                if j == 0:
                    prevx = p.op("vector", lambda e: e.tensor_copy(
                        xsum_c[:, :], xst[:, :]), deps=[tg], sig=True)
                else:
                    prevx = p.op("vector", lambda e: e.tensor_add(
                        xsum_c[:, :], xsum_c[:, :], xst[:, :]),
                        deps=[tg], sig=True)
            txb = p.op("vector", lambda e: e.tensor_copy(
                xsum_bf[:, :], xsum_c[:, :]), deps=[prevx], sig=True)
            mmq = {}
            for i in range(NI):
                tkw = p.dma("gpsimd", wqfb[i % 2][:, :],
                            WA[i * 128:(i + 1) * 128, 6784:6912],
                            deps=[mmq.get(i - 2)] +
                            (c1a_done if i < 2 else []))
                mmq[i] = p.op("tensor", lambda e, i=i: e.matmul(
                    pb[4][0:1, 0:128], xsum_bf[:, i:i + 1], wqfb[i % 2][:, :],
                    start=(i == 0), stop=(i == NI - 1)),
                    deps=[tkw, txb], sig=True)
            tqr = p.op("vector", lambda e: e.tensor_scalar(
                qrow[0:1, :], pb[4][0:1, 0:128], consts[0:1, C_QS:C_QS + 1],
                None, OP.mult), deps=[mmq[NI - 1]], sig=True)
            tb1 = p.dma("sync", r_bounce[0:1, 0:128], qrow[0:1, :], deps=[tqr])
            tb2 = p.dma("sync", qcol[:, :],
                        r_bounce[0:1, 0:128].rearrange("a k -> k a"),
                        deps=[tb1])
            mms = p.op("tensor", lambda e: e.matmul(
                pb[4][0:1, 128:136], qcol[:, :], keys_s[:, :],
                start=True, stop=True), deps=[tb2], sig=True)
            p.op("vector", lambda e: e.tensor_copy(
                paux[0:1, 800:808], pb[4][0:1, 128:136]), deps=[mms])
            p.op("vector", lambda e: e.tensor_reduce(
                paux[0:1, 824:825], paux[0:1, 800:808], AXX, OP.max))
            p.op("vector", lambda e: e.tensor_scalar(
                paux[0:1, 800:808], paux[0:1, 800:808],
                paux[0:1, 824:825], None, OP.subtract))
            tso = p.op("scalar", lambda e: e.activation(
                paux[0:1, 800:808], paux[0:1, 800:808], AF.Exp),
                deps=[(p.msem["vector"], p.mcount["vector"])], sig=True)
            p.op("vector", lambda e: e.tensor_reduce(
                paux[0:1, 824:825], paux[0:1, 800:808], AXX, OP.add),
                deps=[tso])
            p.op("vector", lambda e: e.reciprocal(
                paux[0:1, 825:826], paux[0:1, 824:825]))
            tcoef = p.op("vector", lambda e: e.tensor_scalar(
                paux[0:1, 808:816], paux[0:1, 800:808],
                paux[0:1, 825:826], None, OP.mult), sig=True)
            tb3 = p.dma("sync", r_bounce[1:2, 0:8], paux[0:1, 808:816],
                        deps=[tcoef])
            tb4 = p.dma("sync", ccol[:, :],
                        r_bounce[1:2, 0:8].rearrange("a k -> k a"),
                        deps=[tb3])
            prevc = None
            for rr in range(4):
                mmc = p.op("tensor", lambda e, rr=rr: e.matmul(
                    pb[4][:, 256:257], sel_s[:, rr * 128:(rr + 1) * 128],
                    ccol[:, :], start=True, stop=True),
                    deps=[tb4, t_sel, prevc], sig=True)
                prevc = p.op("vector", lambda e, rr=rr: e.tensor_copy(
                    ce_cols[:, rr:rr + 1], pb[4][:, 256:257]),
                    deps=[mmc], sig=True)
            uprev = {0: None, 1: None}
            for ui, (rb, hf) in enumerate([(r, h) for r in range(4)
                                           for h in range(2)]):
                ub = upb[ui % 2]
                tu = p.dma("gpsimd", ub[:, :],
                           UPT_in[rb * 128:(rb + 1) * 128,
                                  hf * 2048:(hf + 1) * 2048],
                           deps=[uprev[ui % 2]] + (c1a_done if ui < 2 else []))
                tsc = p.op("vector", lambda e, rb=rb, ub=ub: e.tensor_scalar(
                    ub[:, :], ub[:, :], ce_cols[:, rb:rb + 1],
                    None, OP.mult), deps=[tu, prevc], sig=True)
                tws = p.dma("gpsimd", upq[rb * 128:(rb + 1) * 128,
                                          hf * 2048:(hf + 1) * 2048],
                            ub[:, :], deps=[tsc])
                tok_upq.append(tws)
                uprev[ui % 2] = tws
            tok_pre = tcoef
            if "DBGPRE" in phases:
                d_in = dramo("d_in", [128, H * D], F32)
                d_coef = dramo("d_coef", [1, 1024], F32)
                d_ce = dramo("d_ce", [128, 4], F32)
                tok_out.append(p.dma("sync", d_in, state[:, :], deps=[tok_in]))
                tok_out.append(p.dma("sync", d_coef, paux[:, :], deps=[tcoef]))
                tok_out.append(p.dma("sync", d_ce, ce_cols[:, :], deps=[prevc]))
            stp.close()

        tok_b2 = None
        yT = None
        if "B2" in phases:
            yT = sb("yT", [128, H * T], BF16)
            stb2 = stack.enter_context(ExitStack())
            q_half = sb("q_half", [128, 16 * T], BF16, st=stb2)
            grow = sb("grow", [1, 2048], F32, st=stb2)
            dsb = sb("dsb", [128, 2048], F32, st=stb2)
            decb = sb("decb", [128, 2048], BF16, st=stb2)
            xcb4 = [sb(f"xcb4{i}", [128, 512], BF16, st=stb2) for i in range(2)]
            kdb4 = [sb(f"kdb4{i}", [128, 512], BF16, st=stb2) for i in range(2)]
            sdec4 = [sb(f"sdec4{i}", [128, 512], BF16, st=stb2)
                     for i in range(2)]
            pvb4 = [sb(f"pvb4{i}", [128, 512], BF16, st=stb2) for i in range(2)]
            rcol = sb("rcol", [128, 16], F32, st=stb2)
            yscb = [sb(f"yscb{i}", [128, 512], BF16, st=stb2)
                    for i in range(2)]
            yfv = [dsb[:, 0:512], dsb[:, 512:1024]]
            ysbv = [dsb[:, 1024:1536], dsb[:, 1536:2048]]

            def expand(ap_cols, reps=128):
                return ap_cols.unsqueeze(2).broadcast_to(
                    (128, ap_cols.shape[1], reps))

            def expand_tile(ap_tile, k=4):
                return ap_tile.unsqueeze(1).broadcast_to((128, k, 128))

            last_q_use = None
            pb_free = {0: tok_c1a, 1: tok_c1a, 2: tok_c1a, 3: tok_c1a,
                       4: prevc, 5: prevc}
            pbt_free2 = {0: ctx["pbt_free"][0], 1: ctx["pbt_free"][1]}
            dep_q0 = p.qtoks("gpsimd") + [tok_pre]
            prev_tys = None
            prev_tsq = None
            prev_mmD3 = None
            xckd_free = {0: None, 1: None}
            sdec_free = {0: None, 1: None}
            pvb_free = {0: None, 1: None}
            yscb_free = {0: None, 1: None}
            for hh in range(2):
                qtk = []
                for l in range(16):
                    qtk.append(p.dma(
                        "sync", q_half[:, l * T:(l + 1) * T],
                        q_sp[(hh * 16 + l) * 128:(hh * 16 + l + 1) * 128, :],
                        deps=dep_q0 + ([last_q_use] if last_q_use else [])))
                for n in range(NCH):
                    hg0 = hh * 16
                    tgr = p.dma("sync", grow[0:1, :],
                                g_bounce[n:n + 1, hh * 2048:(hh + 1) * 2048],
                                deps=(tok_gb if (hh == 0 and n == 0) else []) +
                                ([prev_mmD3] if prev_mmD3 else []))
                    mmD = []
                    for qd in range(4):
                        mmD.append(p.op("tensor", lambda e, qd=qd: e.matmul(
                            pb[qd][:, :], ones_r[0:1, 0:128],
                            grow[0:1, qd * 512:(qd + 1) * 512],
                            start=True, stop=True),
                            deps=[tgr, pb_free[qd]], sig=True))
                    prev_mmD3 = mmD[3]
                    trl = []
                    for l in range(16):
                        hg = hg0 + l
                        dprev = ([prev_tys, prev_tsq]
                                 if (l == 0 and prev_tys) else [])
                        trl.append(p.op("scalar", lambda e, l=l, hg=hg, n=n:
                                        e.activation(
                                            dsb[:, l * 128:(l + 1) * 128],
                                            pb[l // 4][:, (l % 4) * 128:
                                                       (l % 4 + 1) * 128],
                                            AF.Relu, scale=-1.0,
                                            bias=G_tm[:, n * 32 + hg:
                                                      n * 32 + hg + 1]),
                                        deps=[mmD[l // 4]] + dprev,
                                        par=(1 if l == 0 else 4), sig=True))
                    for qd in range(4):
                        pb_free[qd] = trl[qd * 4 + 3]
                    tex = p.op("scalar", lambda e: e.activation(
                        decb[:, :], dsb[:, :], AF.Exp, scale=-1.0),
                        deps=[trl[-1]], sig=True)
                    tmask = p.op("vector", lambda e: e.tensor_tensor(
                        decb[:, :].rearrange("p (a b) -> p a b", a=16),
                        decb[:, :].rearrange("p (a b) -> p a b", a=16),
                        expand_tile(tri_bf[:, :], 16), OP.mult),
                        deps=[tex, t_tribf], sig=True)
                    for kvpair in range(2):
                        st2 = {}
                        for pi in (0, 1):
                            kvl = 2 * kvpair + pi
                            kv = hh * 4 + kvl
                            hgk = hg0 + 4 * kvl
                            st2[pi] = dict(kvl=kvl, kv=kv, hgk=hgk,
                                           pS=pb[4 + pi], pY1=pb[0 + pi],
                                           pY2=pb[2 + pi])
                        for pi in (0, 1):
                            s = st2[pi]
                            mm = []
                            for m in range(4):
                                mm.append(p.op(
                                    "tensor", lambda e, s=s, n=n, m=m, pi=pi:
                                    e.matmul(
                                        s["pS"][:, m * 128:(m + 1) * 128],
                                        kT[:, s["kv"] * T + n * 128:
                                           s["kv"] * T + (n + 1) * 128],
                                        q_half[:, (4 * s["kvl"] + m) * T +
                                               n * 128:
                                               (4 * s["kvl"] + m) * T +
                                               (n + 1) * 128],
                                        start=True, stop=True),
                                    deps=[pb_free[4 + pi]] + qtk,
                                    sig=(m == 3)))
                            s["mmsc"] = mm[3]
                        for pi in (0, 1):
                            s = st2[pi]
                            s["txc"] = p.op("gpsimd", lambda e, s=s, n=n, pi=pi:
                                            e.tensor_tensor(
                                                xcb4[pi][:, :].rearrange(
                                                    "p (a b) -> p a b", a=4),
                                                expand_tile(v_tm[:, (s["kv"] * 8 + n) * 128:
                                                                 (s["kv"] * 8 + n + 1) * 128]),
                                                expand(dt_tm[:, n * 32 + s["hgk"]:
                                                             n * 32 + s["hgk"] + 4]),
                                                OP.mult),
                                            deps=[xckd_free[pi]], par=2,
                                            sig=True)
                        for pi in (0, 1):
                            s = st2[pi]
                            s["tkd"] = p.op("gpsimd", lambda e, s=s, n=n, pi=pi:
                                            e.tensor_tensor(
                                                kdb4[pi][:, :].rearrange(
                                                    "p (a b) -> p a b", a=4),
                                                expand_tile(k_tm[:, (s["kv"] * 8 + n) * 128:
                                                                 (s["kv"] * 8 + n + 1) * 128]),
                                                expand(edec_tm[:, n * 32 + s["hgk"]:
                                                               n * 32 + s["hgk"] + 4]),
                                                OP.mult), par=2, sig=True)
                        for pi in (0, 1):
                            s = st2[pi]
                            s["tsd"] = p.op("vector", lambda e, s=s, pi=pi:
                                            e.tensor_tensor(
                                                sdec4[pi][:, :], s["pS"][:, :],
                                                decb[:, s["kvl"] * 512:
                                                     (s["kvl"] + 1) * 512],
                                                OP.mult),
                                            deps=[s["mmsc"], tmask,
                                                  sdec_free[pi]],
                                            par=2, sig=True)
                        for pi in (0, 1):
                            s = st2[pi]
                            s["tpv"] = p.op("vector", lambda e, s=s, pi=pi:
                                            e.tensor_copy(
                                                pvb4[pi][:, :],
                                                state[:, s["hgk"] * 128:
                                                      (s["hgk"] + 4) * 128]),
                                            deps=[pvb_free[pi]], par=2,
                                            sig=True)
                        for pi in (0, 1):
                            s = st2[pi]
                            mm = []
                            for m in range(4):
                                mm.append(p.op(
                                    "tensor", lambda e, s=s, n=n, m=m, pi=pi:
                                    e.matmul(
                                        s["pY1"][:, m * 128:(m + 1) * 128],
                                        q_half[:, (4 * s["kvl"] + m) * T +
                                               n * 128:
                                               (4 * s["kvl"] + m) * T +
                                               (n + 1) * 128],
                                        pvb4[pi][:, m * 128:(m + 1) * 128],
                                        start=True, stop=True),
                                    deps=[s["tpv"], pb_free[0 + pi]],
                                    sig=(m == 3)))
                            s["mmy1"] = mm[3]
                        for pi in (0, 1):
                            s = st2[pi]
                            mm = []
                            for m in range(4):
                                mm.append(p.op(
                                    "tensor", lambda e, s=s, m=m, pi=pi: e.matmul(
                                        s["pY2"][:, m * 128:(m + 1) * 128],
                                        sdec4[pi][:, m * 128:(m + 1) * 128],
                                        xcb4[pi][:, m * 128:(m + 1) * 128],
                                        start=True, stop=True),
                                    deps=[s["tsd"], s["txc"],
                                          pb_free[2 + pi]], sig=(m == 3)))
                            s["mmy2"] = mm[3]
                        for pi in (0, 1):
                            s = st2[pi]
                            s["ty1"] = p.op("vector", lambda e, s=s, n=n, pi=pi:
                                            e.tensor_tensor(
                                                ysbv[pi].rearrange(
                                                    "p (a b) -> p a b", a=4),
                                                s["pY1"][:, :].rearrange(
                                                    "p (a b) -> p a b", a=4),
                                                expand(expG_tm[:, n * 32 + s["hgk"]:
                                                               n * 32 + s["hgk"] + 4]),
                                                OP.mult),
                                            deps=[s["mmy1"]], par=2, sig=True)
                        for pi in (0, 1):
                            s = st2[pi]
                            s["ty2"] = p.op("vector", lambda e, s=s, pi=pi:
                                            e.tensor_add(
                                                yfv[pi], ysbv[pi],
                                                s["pY2"][:, :]),
                                            deps=[s["mmy2"]], par=2, sig=True)
                            pb_free[0 + pi] = s["ty1"]
                            pb_free[2 + pi] = s["ty2"]
                        for pi in (0, 1):
                            s = st2[pi]
                            mm = []
                            for m in range(4):
                                mm.append(p.op(
                                    "tensor", lambda e, s=s, m=m, pi=pi: e.matmul(
                                        s["pS"][:, m * 128:(m + 1) * 128],
                                        kdb4[pi][:, m * 128:(m + 1) * 128],
                                        xcb4[pi][:, m * 128:(m + 1) * 128],
                                        start=True, stop=True),
                                    deps=[s["tsd"], s["tkd"]], sig=(m == 3)))
                            s["mmcs"] = mm[3]
                            sdec_free[pi] = s["mmy2"]
                            xckd_free[pi] = s["mmcs"]
                        for pi in (0, 1):
                            s = st2[pi]
                            tst = None
                            for m in range(4):
                                tst = p.op("vector", lambda e, s=s, n=n, m=m, pi=pi:
                                           e.scalar_tensor_tensor(
                                               state[:, (s["hgk"] + m) * 128:
                                                     (s["hgk"] + m + 1) * 128],
                                               state[:, (s["hgk"] + m) * 128:
                                                     (s["hgk"] + m + 1) * 128],
                                               dec_bc[:, n * 32 + s["hgk"] + m:
                                                      n * 32 + s["hgk"] + m + 1],
                                               s["pS"][:, m * 128:(m + 1) * 128],
                                               OP.mult, OP.add),
                                           deps=[s["mmcs"]], par=2,
                                           sig=(m == 3))
                            pb_free[4 + pi] = tst
                            pvb_free[pi] = s["mmy1"]
                        for pi in (0, 1):
                            s = st2[pi]
                            s["tsq"] = p.op("scalar", lambda e, s=s, pi=pi:
                                            e.activation(ysbv[pi], yfv[pi],
                                                         AF.Square),
                                            deps=[s["ty2"]], par=2, sig=True)
                        for pi in (0, 1):
                            s = st2[pi]
                            trd = None
                            for m in range(4):
                                trd = p.op("vector", lambda e, s=s, m=m, pi=pi:
                                           e.tensor_reduce(
                                               rcol[:, 8 * pi + m:
                                                    8 * pi + m + 1],
                                               ysbv[pi][:, m * 128:
                                                        (m + 1) * 128],
                                               AXX, OP.add),
                                           deps=[s["tsq"]], par=2,
                                           sig=(m == 3))
                            s["trd"] = trd
                        for pi in (0, 1):
                            s = st2[pi]
                            s["tms"] = p.op("vector", lambda e, s=s, pi=pi:
                                            e.tensor_scalar(
                                                rcol[:, 8 * pi:8 * pi + 4],
                                                rcol[:, 8 * pi:8 * pi + 4],
                                                consts[:, C_I128:C_I128 + 1],
                                                None, OP.mult),
                                            deps=[s["trd"]], par=2, sig=True)
                        for pi in (0, 1):
                            s = st2[pi]
                            s["tsr"] = p.op("scalar", lambda e, s=s, pi=pi:
                                            e.activation(
                                                rcol[:, 8 * pi + 4:8 * pi + 8],
                                                rcol[:, 8 * pi:8 * pi + 4],
                                                AF.Sqrt,
                                                bias=consts[:, C_EPS:C_EPS + 1]),
                                            deps=[s["tms"]], par=2, sig=True)
                        for pi in (0, 1):
                            s = st2[pi]
                            s["trc"] = p.op("vector", lambda e, s=s, pi=pi:
                                            e.reciprocal(
                                                rcol[:, 8 * pi:8 * pi + 4],
                                                rcol[:, 8 * pi + 4:8 * pi + 8]),
                                            deps=[s["tsr"]], par=2, sig=True)
                        for pi in (0, 1):
                            s = st2[pi]
                            s["tys"] = p.op("vector", lambda e, s=s, pi=pi:
                                            e.tensor_tensor(
                                                yscb[pi][:, :].rearrange(
                                                    "p (a b) -> p a b", a=4),
                                                yfv[pi].rearrange(
                                                    "p (a b) -> p a b", a=4),
                                                expand(rcol[:, 8 * pi:
                                                            8 * pi + 4]),
                                                OP.mult),
                                            deps=[s["trc"], yscb_free[pi]],
                                            par=2, sig=True)
                            prev_tys = s["tys"]
                            prev_tsq = s["tsq"]
                        for pi in (0, 1):
                            s = st2[pi]
                            mm = []
                            for m in range(4):
                                mm.append(p.op(
                                    "tensor", lambda e, s=s, m=m, pi=pi:
                                    e.transpose(
                                        pbt[pi][:, m * 128:(m + 1) * 128],
                                        yscb[pi][:, m * 128:(m + 1) * 128],
                                        ident[:, :]),
                                    deps=[s["tys"], pbt_free2[pi], t_id],
                                    sig=(m == 3)))
                            s["mmt"] = mm[3]
                        for pi in (0, 1):
                            s = st2[pi]
                            tyev = p.op("scalar", lambda e, s=s, n=n, pi=pi:
                                        e.copy(
                                            yT[:, :].rearrange(
                                                "p (h t) -> p h t", h=32)[
                                                :, s["hgk"]:s["hgk"] + 4,
                                                n * 128:(n + 1) * 128],
                                            pbt[pi][:, 0:512].rearrange(
                                                "p (a b) -> p a b", a=4)),
                                        deps=[s["mmt"]], par=2, sig=True)
                            pbt_free2[pi] = tyev
                            yscb_free[pi] = s["mmt"]
                            last_q_use = s["mmy1"]
            tok_b2 = last_q_use
            if "DBGB2" in phases:
                d_yT = dramo("d_yT", [128, H * T], BF16)
                tok_out.append(p.dma("sync", d_yT, yT[:, :], deps=[tok_b2]))
            stb2.close()

        if "C2" in phases:
            stack_pbt.close()
            stack_r.close()
            pb67 = [psum(f"pb{i + 6}", [128, 512], F32) for i in range(2)]
            banks = pb + pb67
            stc2 = stack.enter_context(ExitStack())
            xnT2 = sb("xnT2", [128, NI * T], BF16, st=stc2)
            hTr = sb("hTr", [128, 4 * T], BF16, st=stc2)
            wcb = [sb(f"wcbuf{i}", [128, 512], BF16, st=stc2) for i in range(4)]
            xt = [sb(f"xt{i}", [128, 512], F32, st=stc2) for i in range(2)]
            ostage = [sb(f"ost{i}", [128, 512], F32, st=stc2) for i in range(2)]
            gsb3 = [sb(f"gsb3{i}", [128, 1024], BF16, st=stc2)
                    for i in range(2)]

            d_xnT2 = dramo("d_xnT2", [128, NI * T], BF16) \
                if "DBGC2" in phases else None
            d_bank0 = dramo("d_bank0", [128, 512], F32) \
                if "DBGC2" in phases else None
            dbg_m0tok = None
            d_bank = dramo("d_bank", [128, 512], F32) \
                if "DBGC2" in phases else None
            dep_c2 = [tok_b2] + p.qtoks("gpsimd") + p.qtoks("sync")
            if "C1b" in phases:
                prevgs = None
                gsfree = {0: None, 1: None}
                for m in range(32):
                    tg = p.dma("scalar", gsb3[m % 2][:, :],
                               gs_sp[m * 128:(m + 1) * 128, :],
                               deps=(dep_c2 if m < 2 else []) +
                               ([gsfree[m % 2]] if gsfree[m % 2] else []))
                    prevgs = p.op("vector", lambda e, m=m:
                                  e.scalar_tensor_tensor(
                                      yT[:, m * T:(m + 1) * T],
                                      yT[:, m * T:(m + 1) * T],
                                      gnorm[:, 0:1], gsb3[m % 2][:, :],
                                      OP.mult, OP.mult),
                                  deps=[tg, tok_b2], par=2, sig=True)
                    gsfree[m % 2] = prevgs
                tok_b2 = prevgs
                dep_c2 = [tok_b2] + dep_c2
            t_x2 = []
            for i in range(NI):
                t_x2.append(p.dma("sync", xnT2[:, i * T:(i + 1) * T],
                                  xn_tm[:, i * 128:(i + 1) * 128],
                                  deps=dep_c2 if i == 0 else [],
                                  transpose=True))
            t_h2 = []
            for r in range(4):
                t_h2.append(p.dma("sync", hTr[:, r * T:(r + 1) * T],
                                  hT_sp[r * 128:(r + 1) * 128, :],
                                  deps=dep_c2 if r == 0 else []))
            lhs_ready = [tok_b2] + t_x2[-8:] + t_h2

            def lhs_ap(m, tt):
                if m < 32:
                    return yT[:, m * T + tt * 128:m * T + (tt + 1) * 128]
                if m < 64:
                    i = m - 32
                    return xnT2[:, i * T + tt * 128:i * T + (tt + 1) * 128]
                r = m - 64
                return hTr[:, r * T + tt * 128:r * T + (tt + 1) * 128]

            bank_free3 = {i: None for i in range(8)}
            xt_last = {}
            ost_last = {}
            mlist = []
            if 0 in c2_parts:
                mlist += list(range(32))
            if 1 in c2_parts:
                mlist += list(range(32, 64))
            if 2 in c2_parts:
                mlist += list(range(64, 68))
            mmW = {}
            for ob in range(8):
                stop_toks = {}
                for mi, m in enumerate(mlist):
                    wb = wcb[mi % 4]
                    depw = [mmW.get((ob, mi - 4))] if mi >= 4 else (
                        [mmW.get((ob - 1, len(mlist) - 4 + mi))] if ob > 0
                        else list(dep_c2))
                    if m < 64:
                        src = (WC[m * 128:(m + 1) * 128,
                                  ob * 512:(ob + 1) * 512] if m < 32 else
                               WC[(m - 32) * 128:(m - 31) * 128,
                                  4096 + ob * 512:4096 + (ob + 1) * 512])
                        tkw = p.dma("gpsimd", wb[:, :], src, deps=depw)
                    else:
                        r = m - 64
                        tkw = p.dma("sync", wb[:, :],
                                    upq[r * 128:(r + 1) * 128,
                                        ob * 512:(ob + 1) * 512],
                                    deps=depw + (tok_upq if m == 64 else []))
                    last = None
                    first = mi == 0
                    lastm = mi == len(mlist) - 1
                    for tt in range(8):
                        dmm = [tkw]
                        if first:
                            dmm += [bank_free3[tt]] + lhs_ready
                        last = p.op("tensor", lambda e, m=m, tt=tt, wb=wb,
                                    first=first, lastm=lastm:
                                    e.matmul(banks[tt][:, :], lhs_ap(m, tt),
                                             wb[:, :], start=first,
                                             stop=lastm),
                                    deps=dmm, sig=(tt == 7) or lastm)
                        if lastm:
                            stop_toks[tt] = last
                        if ob == 0 and mi == 0 and tt == 7:
                            dbg_m0tok = last
                    mmW[(ob, mi)] = last
                if ob == 0 and d_bank is not None:
                    tdbg = p.op("vector", lambda e: e.tensor_copy(
                        ostage[0][:, :], banks[0][:, :]),
                        deps=[stop_toks[0]], sig=True)
                    tok_out.append(p.dma("sync", d_bank, ostage[0][:, :],
                                         deps=[tdbg]))
                    tok_out.append(p.dma("sync", d_xnT2, xnT2[:, :],
                                         deps=[tdbg]))
                    xt_last[0] = tdbg
                    ost_last[0] = tok_out[-2]
                if ob == 0 and d_bank0 is not None:
                    tdb0 = p.op("vector", lambda e: e.tensor_copy(
                        ostage[1][:, :], banks[0][:, :]),
                        deps=[dbg_m0tok], sig=True)
                    tok_out.append(p.dma("sync", d_bank0, ostage[1][:, :],
                                         deps=[tdb0]))
                    xt_last[1] = tdb0
                    ost_last[1] = tok_out[-1]
                for tt in range(8):
                    bi = tt % 2
                    xt_war = xt_last.get(bi)
                    tx_ = p.dma("sync", xt[bi][:, :],
                                x_in[tt * 128:(tt + 1) * 128,
                                     ob * 512:(ob + 1) * 512],
                                deps=([bank_free3[tt]] if ob > 0 else
                                      list(dep_c2)) +
                                ([xt_war] if xt_war else []))
                    ost_war = ost_last.get(bi)
                    tadd = p.op("vector", lambda e, tt=tt, bi=bi:
                                e.tensor_add(ostage[bi][:, :],
                                             banks[tt][:, :], xt[bi][:, :]),
                                deps=[tx_, stop_toks[tt]] +
                                ([ost_war] if ost_war else []), sig=True)
                    tw = p.dma("scalar" if tt % 2 == 0 else "sync",
                               out_t[tt * 128:(tt + 1) * 128,
                                     ob * 512:(ob + 1) * 512],
                               ostage[bi][:, :], deps=[tadd])
                    tok_out.append(tw)
                    xt_last[bi] = tadd
                    ost_last[bi] = tw
                    bank_free3[tt] = tadd

        if "C2" not in phases:
            zb = sb("zb", [128, HID], F32)
            tz = p.op("vector", lambda e: e.memset(zb[:, :], 0.0), sig=True)
            for t in range(8):
                tok_out.append(p.dma(
                    "sync", out_t[t * 128:(t + 1) * 128, :], zb[:, :],
                    deps=[tz, tok_A1_last, tok_b2, tok_cc]))
        p.op("sync", lambda e: e.nop(),
             deps=tok_out[-24:] + p.qtoks("scalar") + p.qtoks("sync") +
             p.qtoks("gpsimd"))

        p.build(blk)
    return nc


def _evac_osub(p, sub, bank0, bank1, mmtoks, evf, evf2, rsc, evb, spill_free,
               cosb, sinb, t_cos, t_sin, kT, k_tm, v_tm, ident, pbt, ctx,
               q_sp, hT_sp, g_fm, dt_fm, dtbias, acol, t_aneg, evac_prev,
               pswapT=None, t_psw=None, pb45=None):
    tm0, tm1 = mmtoks
    qsub = sub < 32
    ksub = 32 <= sub < 40
    vsub = 40 <= sub < 48
    wsub = sub == 48
    dsub = 49 <= sub < 53

    if qsub or ksub:
        c0 = p.op("vector", lambda e: e.tensor_copy(evf[:, 0:512], bank0[:, :]),
                  deps=[tm0], sig=False)
        c1 = p.op("vector", lambda e: e.tensor_copy(evf[:, 512:1024], bank1[:, :]),
                  deps=[tm1], sig=True)
        bx = evb[0]
        rfree = ctx.get("rope_free")
        txs = p.op("vector", lambda e, bx=bx: e.tensor_mul(
            bx[:, :], evf[:, :], sinb[:, :]),
            deps=[t_sin, spill_free[0]], sig=True)
        spill_free[0] = txs
        mr0 = p.op("tensor", lambda e, bx=bx: e.matmul(
            pb45[0][:, :], pswapT[:, :], bx[:, 0:512], start=True, stop=True),
            deps=[txs, t_psw, rfree], sig=True)
        mr1 = p.op("tensor", lambda e, bx=bx: e.matmul(
            pb45[1][:, :], pswapT[:, :], bx[:, 512:1024],
            start=True, stop=True), sig=True)
        p.op("vector", lambda e: e.tensor_mul(evf2[:, :], evf[:, :], cosb[:, :]),
             deps=[t_cos])
        dst = evb[1]
        a0 = p.op("vector", lambda e, dst=dst: e.tensor_add(
            dst[:, 0:512], evf2[:, 0:512], pb45[0][:, :]),
            deps=[mr0, spill_free[1]], sig=True)
        rope_done = p.op("vector", lambda e, dst=dst: e.tensor_add(
            dst[:, 512:1024], evf2[:, 512:1024], pb45[1][:, :]),
            deps=[mr1], sig=True)
        ctx["rope_free"] = rope_done
        if qsub:
            tw = p.dma("gpsimd", q_sp[sub * 128:(sub + 1) * 128, :], dst[:, :],
                       deps=[rope_done])
            spill_free[1] = tw
            return c1
        kv = sub - 32
        ck = p.op("vector", lambda e, kv=kv, dst=dst: e.tensor_copy(
            kT[:, kv * T:(kv + 1) * T], dst[:, :]), deps=[rope_done], sig=True)
        spill_free[1] = ck
        last = None
        for n in range(8):
            pk = n % 2
            t1 = p.op("tensor", lambda e, kv=kv, n=n, pk=pk: e.transpose(
                pbt[pk][:, 0:128],
                kT[:, kv * T + n * 128:kv * T + (n + 1) * 128], ident[:, :]),
                deps=[ck, ctx["pbt_free"][pk]], sig=True)
            t2 = p.op("vector", lambda e, kv=kv, n=n, pk=pk: e.tensor_copy(
                k_tm[:, (kv * 8 + n) * 128:(kv * 8 + n + 1) * 128],
                pbt[pk][:, 0:128]), deps=[t1], sig=True)
            ctx["pbt_free"][pk] = t2
            last = t2
        ctx["ktm_done"] = last
        return c1

    if vsub:
        kv = sub - 40
        c0 = p.op("vector", lambda e: e.tensor_copy(evf[:, 0:512], bank0[:, :]),
                  deps=[tm0])
        c1 = p.op("vector", lambda e: e.tensor_copy(evf[:, 512:1024], bank1[:, :]),
                  deps=[tm1], sig=True)
        b = evb[0]
        cb = p.op("vector", lambda e, b=b: e.tensor_copy(b[:, :], evf[:, :]),
                  deps=[spill_free[0]], sig=True)
        spill_free[0] = cb
        last = None
        for n in range(8):
            pk = n % 2
            t1 = p.op("tensor", lambda e, n=n, pk=pk, b=b: e.transpose(
                pbt[pk][:, 128:256], b[:, n * 128:(n + 1) * 128], ident[:, :]),
                deps=[cb, ctx["pbt_free"][pk]], sig=True)
            t2 = p.op("vector", lambda e, kv=kv, n=n, pk=pk: e.tensor_copy(
                v_tm[:, (kv * 8 + n) * 128:(kv * 8 + n + 1) * 128],
                pbt[pk][:, 128:256]), deps=[t1], sig=True)
            ctx["pbt_free"][pk] = t2
            last = t2
        spill_free[0] = last
        ctx["vtm_done"] = last
        return c1

    if wsub:
        c0 = p.op("vector", lambda e: e.tensor_copy(evf[0:32, 0:512],
                                                    bank0[0:32, :]), deps=[tm0])
        c1 = p.op("vector", lambda e: e.tensor_copy(evf[0:32, 512:1024],
                                                    bank1[0:32, :]),
                  deps=[tm1], sig=True)
        p.op("vector", lambda e: e.tensor_scalar(
            evf[0:32, :], evf[0:32, :], dtbias[:, 0:1], None, OP.add))
        ta = p.op("vector", lambda e: e.tensor_scalar(
            evf2[0:32, :], evf[0:32, :], 0.0, None, OP.max), sig=True)
        a0 = p.op("scalar", lambda e: e.activation(
            rsc[0:32, :], evf[0:32, :], AF.Abs), deps=[ta], sig=True)
        p.op("scalar", lambda e: e.activation(
            rsc[0:32, :], rsc[0:32, :], AF.Exp, scale=-1.0))
        a2 = p.op("scalar", lambda e: e.activation(
            rsc[0:32, :], rsc[0:32, :], AF.Ln, bias=1.0), sig=True)
        a4 = p.op("vector", lambda e: e.tensor_add(
            dt_fm[:, :], rsc[0:32, :], evf2[0:32, :]), deps=[a2], sig=True)
        a5 = p.op("vector", lambda e: e.tensor_scalar(
            g_fm[:, :], dt_fm[:, :], acol[:, 0:1], None, OP.mult),
            deps=[t_aneg], sig=True)
        return a5

    if dsub:
        rr = sub - 49
        b = evb[rr % 2]
        c0 = p.op("vector", lambda e, b=b: e.tensor_copy(b[:, 0:512], bank0[:, :]),
                  deps=[tm0, spill_free[rr % 2]])
        c1 = p.op("vector", lambda e, b=b: e.tensor_copy(b[:, 512:1024],
                                                         bank1[:, :]),
                  deps=[tm1], sig=True)
        tw = p.dma("gpsimd", hT_sp[rr * 128:(rr + 1) * 128, :], b[:, :], deps=[c1])
        spill_free[rr % 2] = tw
        return c1
    raise AssertionError(sub)


def _rope_tables():
    inv_freq = 1.0 / ROPE_THETA ** (np.arange(0, D, 2, dtype=np.float64) / D)
    ang = np.arange(L, dtype=np.float64)[:, None] * inv_freq
    cos = np.concatenate([np.cos(ang), np.cos(ang)], axis=-1)
    sin = np.concatenate([np.sin(ang), np.sin(ang)], axis=-1)
    return cos.astype(np.float32), sin.astype(np.float32)


def _pswapT():
    P = np.zeros((128, 128), np.float32)
    for dst in range(128):
        if dst < 64:
            P[dst + 64, dst] = -1.0
        else:
            P[dst - 64, dst] = 1.0
    return P.astype(ml_dtypes.bfloat16)


def host_inputs(inputs):
    x = np.ascontiguousarray(np.asarray(inputs["L0_attn_hidden_states"],
                                        np.float32)[0])
    Wq = np.asarray(inputs["L0_attn_Wq"], np.float32)
    Wk = np.asarray(inputs["L0_attn_Wk"], np.float32)
    Wv = np.asarray(inputs["L0_attn_Wv"], np.float32)
    Wo = np.asarray(inputs["L0_attn_Wo"], np.float32)
    Wg = np.asarray(inputs["L0_attn_Wg"], np.float32)
    Win = np.asarray(inputs["L0_attn_Win"], np.float32)
    dt_bias = np.asarray(inputs["L0_attn_dt_bias"], np.float32)
    A_log = np.asarray(inputs["L0_attn_A_log_bias"], np.float32)
    gnorm = np.asarray(inputs["L0_attn_gnorm_w"], np.float32)
    W_org = np.asarray(inputs["L0_ffn_W_org"], np.float32)
    down = np.asarray(inputs["L0_ffn_lora_down"], np.float32)
    up = np.asarray(inputs["L0_ffn_lora_up"], np.float32)
    keys = np.asarray(inputs["L0_ffn_lora_keys"], np.float32)
    Wqf = np.asarray(inputs["L0_ffn_Wq"], np.float32)

    WA = np.ascontiguousarray(np.concatenate([
        Wq.T, Wk.T, Wv.T,
        np.concatenate([Win.T, np.zeros((HID, 96), np.float32)], axis=1),
        down.transpose(1, 0, 2).reshape(HID, 512),
        Wqf.T], axis=1))
    assert WA.shape == (HID, WA_COLS)
    cos, sin = _rope_tables()
    tri = np.ascontiguousarray(np.triu(np.ones((128, 128), np.float32)))
    Ihead = np.zeros((32, HID), np.float32)
    for h in range(32):
        Ihead[h, h * 128:(h + 1) * 128] = 1.0
    e127 = np.zeros((128, 1), np.float32)
    e127[127, 0] = 1.0
    Sel = np.zeros((8, 512), np.float32)
    for rr in range(4):
        for pp in range(128):
            Sel[2 * rr + pp // 64, rr * 128 + pp] = 1.0

    shared = {
        "WA": WA,
        "WG": np.ascontiguousarray(Wg.T),
        "WC": np.ascontiguousarray(np.concatenate([Wo.T, W_org.T], axis=1)),
        "UPT": np.ascontiguousarray(up.reshape(512, HID)),
        "keysT": np.ascontiguousarray(keys.T),
        "dt_bias": np.ascontiguousarray(dt_bias.reshape(32, 1)),
        "A_log": np.ascontiguousarray(A_log.reshape(32, 1)),
        "gnorm": np.ascontiguousarray(gnorm.reshape(128, 1)),
        "trimask": tri, "Ihead": Ihead, "e127": e127, "Sel": Sel,
        "ident_in": np.eye(128, dtype=ml_dtypes.bfloat16),
        "pswapT": _pswapT(),
        "identf_in": np.eye(128, dtype=np.float32),
    }
    in_maps = []
    for c in range(NCORE):
        m = dict(shared)
        m["x_c"] = np.ascontiguousarray(x[c * T:(c + 1) * T])
        m["costab"] = np.ascontiguousarray(cos[c * T:(c + 1) * T].T)
        m["sintab"] = np.ascontiguousarray(sin[c * T:(c + 1) * T].T)
        cm = np.zeros((1, 256), np.float32)
        for j in range(8):
            if j < c:
                cm[0, j * 32:(j + 1) * 32] = 1.0
        m["cmflat"] = cm
        m["omcmflat"] = np.ascontiguousarray(1.0 - cm)
        in_maps.append(m)
    return in_maps


def kernel(**inputs):
    in_maps = host_inputs(inputs)
    nc = build_nc()
    res = run_bass_kernel_spmd(nc, in_maps, core_ids=list(range(NCORE)))
    out = np.concatenate([np.asarray(res.results[c]["out"], np.float32)
                          for c in range(NCORE)], axis=0)
    return out.reshape(1, L, HID)
